# version 02
# baseline (speedup 1.0000x reference)
import contextlib
import numpy as np

import concourse.bass as bass
import concourse.mybir as mybir
from concourse.bass_utils import run_bass_kernel_spmd

F32 = mybir.dt.float32
BF16 = mybir.dt.bfloat16
AF = mybir.ActivationFunctionType
OP = mybir.AluOpType
AX = mybir.AxisListType

B, L, D = 4, 4096, 2048
NH, DH, DHR, DC, DCQ = 16, 128, 64, 512, 1536
HID = 8192
OWN = 2048
NG = L // 512
NGO = OWN // 512
NT = OWN // 128
DT = D // 128
KQ = DCQ // 128
KC = DC // 128
NHC = HID // 128
SCALE = DH ** -0.5
EXP_SHIFT = -12.0
EPS = 1e-5
N_ENG = 5
DEBUG = False
P1A_PARTS = ("qc", "c", "kr")


class Sem:
    def __init__(self, handle):
        self.h = handle
        self.n = 0

    def inc(self, k=1):
        self.n += k
        return self.n


def build_program(stop_after=None):
    nc = bass.Bass("TRN2", target_bir_lowering=False)
    sched = {e: [] for e in ("sync", "tensor", "vector", "scalar", "gpsimd")}

    def emit(e, fn):
        sched[e].append(fn)

    def inp(name, shape, dtype=F32):
        return nc.dram_tensor(name, shape, dtype, kind="ExternalInput")

    xb = inp("xb", [L, D])
    wdq = inp("wdq", [D, DCQ])
    wuq = inp("wuq", [DCQ, NH * DH])
    wqr = inp("wqr", [DCQ, NH * DHR])
    wdkv = inp("wdkv", [D, DC])
    wuk = inp("wuk", [DC, NH * DH])
    wkr = inp("wkr", [D, NH * DHR])
    wuv = inp("wuv", [DC, NH * DH])
    wo = inp("wo", [NH * DH, D])
    gwT = inp("gwT", [D, HID])
    vwT = inp("vwT", [D, HID])
    owT = inp("owT", [HID, D])
    gb = inp("gb", [1, HID])
    vb = inp("vb", [1, HID])
    ob = inp("ob", [1, D])
    cosk = inp("cosk", [128, L])
    sink = inp("sink", [128, L])
    cosq = inp("cosq", [128, OWN])
    sinq = inp("sinq", [128, OWN])

    y_out = nc.dram_tensor("y", [OWN, D], F32, kind="ExternalOutput")
    dbg_out = None
    dtw = {}
    if DEBUG:
        dbg_out = nc.dram_tensor("dbg", [128, 256], F32, kind="ExternalOutput")

    ln_hbm = nc.dram_tensor("ln_s", [L, D], BF16, kind="Internal")
    qcT_hbm = nc.dram_tensor("qcT_s", [128, KQ * OWN], BF16, kind="Internal")
    cT_hbm = nc.dram_tensor("cT_s", [128, KC * L], BF16, kind="Internal")
    k1_hbm = nc.dram_tensor("k1_s", [128, NH * L], BF16, kind="Internal")
    k2_hbm = nc.dram_tensor("k2_s", [64, NH * L], BF16, kind="Internal")
    v_hbm = nc.dram_tensor("v_s", [128, (L // 128) * D], BF16, kind="Internal")
    qT_hbm = nc.dram_tensor("qT_s", [192, NH * OWN], BF16, kind="Internal")
    r_hbm = nc.dram_tensor("r_s", [OWN, D], F32, kind="Internal")
    xq_hbm = nc.dram_tensor("xq_s", [OWN, D], BF16, kind="Internal")
    gv_hbm = nc.dram_tensor("gv_s", [OWN, HID], BF16, kind="Internal")
    gvq_hbm = nc.dram_tensor("gvq_s", [128, NHC * OWN], BF16, kind="Internal")
    tow1_hbm = nc.dram_tensor("tow1_s", [128, NHC * 1024], BF16, kind="Internal")

    if DEBUG:
        for nm, t in (("ln", ln_hbm), ("qcT", qcT_hbm), ("cT", cT_hbm),
                      ("k1", k1_hbm), ("k2", k2_hbm), ("v", v_hbm),
                      ("qT", qT_hbm), ("r", r_hbm), ("xq", xq_hbm),
                      ("gv", gv_hbm), ("gvq", gvq_hbm)):
            shp = [t.shape[0], t.shape[1]] if hasattr(t, "shape") else None
            dtw[nm] = (nc.dram_tensor("d_" + nm, list(t.shape),
                                      t.dtype, kind="ExternalOutput"), t)

    ones_row = nc.inline_tensor(np.ones((1, 128), dtype=np.float32), "ones_row_c")
    iden128 = nc.inline_tensor(np.eye(128, dtype=np.float32), "iden_c")

    root = contextlib.ExitStack()

    def mkstack():
        return root.enter_context(contextlib.ExitStack())

    phase_stacks = []

    def open_stack():
        s = contextlib.ExitStack()
        phase_stacks.append(s)
        return s

    def sb(stack, name, shape, dtype):
        return stack.enter_context(nc.sbuf_tensor(name, shape, dtype))

    S0 = open_stack()
    CONST = sb(S0, "CONST", [128, 8], F32)
    ONE1 = sb(S0, "ONE1", [1, 128], F32)
    IDEN = sb(S0, "IDEN", [128, 128], F32)
    RW8 = sb(S0, "RW8", [128, 8], F32)
    RW8b = sb(S0, "RW8b", [128, 8], F32)
    COL = sb(S0, "COL", [128, 12], F32)
    SPC = sb(S0, "SPC", [128, 4], F32)
    ONES8 = sb(S0, "ONES8", [128, 8], F32)
    GMXb = sb(S0, "GMXb", [128, NT], F32)
    GMX2 = sb(S0, "GMX2", [128, NT], F32)
    GX = sb(S0, "GX", [128, NT], F32)
    GMX = sb(S0, "GMX", [128, NT], F32)
    SGc = sb(S0, "SGc", [128, NT], F32)
    SVc = sb(S0, "SVc", [128, NT], F32)
    SQc = sb(S0, "SQc", [128, NT], F32)
    SOUTc = sb(S0, "SOUTc", [128, NT], F32)
    WACC = sb(S0, "WACC", [128, 8], F32)
    GAM = sb(S0, "GAM", [128, 8], F32)
    GAM2 = sb(S0, "GAM2", [128, 2], F32)
    TRASH = sb(S0, "TRASH", [128, 2048], F32)
    OUTB = sb(S0, "OUTB", [128, D], F32)

    PSL = [S0.enter_context(nc.psum_tensor(f"ps{i}", [128, 512], F32))
           for i in range(8)]

    def mksem(name):
        return Sem(S0.enter_context(nc.semaphore(name)))

    sT = mksem("sT")
    sV = mksem("sV")
    sC = mksem("sC")
    dL = mksem("dL")
    dS = mksem("dS")
    dG = mksem("dG")
    BAR = mksem("BAR")

    ph_no = [0]

    def barrier():
        ph_no[0] += 1
        tgt = N_ENG * ph_no[0]
        flush = {"sync": dL, "scalar": dS, "gpsimd": dG}
        for e in ("sync", "tensor", "vector", "scalar", "gpsimd"):
            fs = flush.get(e)
            fn_n = fs.n if fs else 0
            def f(eng, fs=fs, fn_n=fn_n, tgt=tgt):
                if fs is not None and fn_n > 0:
                    eng.wait_ge(fs.h, fn_n)
                eng.nop().then_inc(BAR.h, 1)
                eng.wait_ge(BAR.h, tgt)
            emit(e, f)
        BAR.inc(N_ENG)

    def dma(qe, qs, out_ap, in_ap, transpose=False):
        n = qs.inc(16)
        def f(eng, out_ap=out_ap, in_ap=in_ap, transpose=transpose):
            eng.dma_start(out_ap, in_ap, transpose=transpose).then_inc(qs.h, 16)
        emit(qe, f)
        return n

    def wait(e, sem, n):
        if n and n > 0:
            def f(eng, n=n, sem=sem):
                eng.wait_ge(sem.h, n)
            emit(e, f)

    def run(e, fn, sem=None):
        if sem is None:
            emit(e, fn)
            return None
        n = sem.inc()
        def f(eng, fn=fn, sem=sem):
            fn(eng).then_inc(sem.h, 1)
        emit(e, f)
        return n

    def dbg_snap(*names):
        if not DEBUG:
            return
        for nm in names:
            dst, srct = dtw[nm]
            dma("sync", dL, dst[:, :], srct[:, :])

    def spacer(v):
        v.tensor_copy(SPC[:, 0:4], CONST[:, 0:4])

    def recip_col(dst_col, src_col, scale=None):
        def f(v, dst=dst_col, src=src_col, scale=scale):
            v.tensor_scalar(RW8b[:, :], ONES8[:, :], src, None, OP.mult)
            spacer(v)
            v.reciprocal(RW8[:, :], RW8b[:, :])
            spacer(v)
            if scale is None:
                return v.tensor_copy(dst, RW8[:, 0:1])
            return v.tensor_scalar(dst, RW8[:, 0:1], float(scale), None, OP.mult)
        return f

    def ln_tile(src_ap, dst_ap, dst2_ap=None):
        wait("vector", sC, sC.n)
        wait("scalar", sV, sV.n)
        def fv(v):
            v.reduce_sum(COL[:, 0:1], src_ap, axis=AX.X)
            spacer(v)
            return v.tensor_scalar(COL[:, 2:3], COL[:, 0:1], 1.0 / D, None, OP.mult)
        run("vector", fv)
        nsq = run("scalar", lambda s: s.activation(
            TRASH[:, :], src_ap, AF.Square, accum_out=COL[:, 1:2]), sC)
        wait("vector", sC, nsq)
        def fv2(v):
            v.tensor_scalar(COL[:, 4:5], COL[:, 1:2], 1.0 / D, None, OP.mult)
            v.tensor_tensor(COL[:, 3:4], COL[:, 2:3], COL[:, 2:3], OP.mult)
            spacer(v)
            v.tensor_tensor(COL[:, 5:6], COL[:, 4:5], COL[:, 3:4], OP.subtract)
            spacer(v)
            return v.tensor_scalar(COL[:, 6:7], COL[:, 5:6], EPS, None, OP.add)
        run("vector", fv2)
        nv2 = run("vector", lambda v: v.nop(), sV)
        wait("scalar", sV, nv2)
        nst = run("scalar", lambda s: s.activation(
            COL[:, 8:9], COL[:, 6:7], AF.Sqrt), sC)
        wait("vector", sC, nst)
        run("vector", recip_col(COL[:, 9:10], COL[:, 8:9]))
        def fng(v):
            spacer(v)
            v.tensor_tensor(COL[:, 10:11], COL[:, 2:3], COL[:, 9:10], OP.mult)
            spacer(v)
            return v.tensor_scalar(COL[:, 11:12], COL[:, 10:11], -1.0, None, OP.mult)
        nng = run("vector", fng, sV)
        wait("scalar", sV, nng)
        nln = run("scalar", lambda s: s.activation(
            dst_ap, src_ap, AF.Identity, bias=COL[:, 11:12], scale=COL[:, 9:10]), sC)
        if dst2_ap is not None:
            nln = run("scalar", lambda s: s.activation(
                dst2_ap, src_ap, AF.Identity,
                bias=COL[:, 11:12], scale=COL[:, 9:10]), sC)
        return nln

    def g0(g):
        g.memset(CONST[:, 0:1], EXP_SHIFT)
        g.memset(CONST[:, 1:2], 0.0)
        g.memset(CONST[:, 2:3], 1.0)
        g.memset(CONST[:, 3:4], -128.0)
        g.memset(CONST[:, 4:5], 1e-5)
        g.memset(WACC[:, :], 0.0)
        g.memset(GMX[:, :], 0.0)
        g.memset(GMXb[:, :], 0.0)
        g.memset(ONES8[:, :], 1.0)
        return g.nop()
    run("gpsimd", g0)
    dma("sync", dL, ONE1[:, :], ones_row[:, :])
    dma("sync", dL, IDEN[:, :], iden128[:, :])

    S1 = open_stack()
    XT = sb(S1, "XT", [128, 2 * D], F32)
    LNB = sb(S1, "LNB", [128, D], BF16)
    LNT = sb(S1, "LNT", [128, DT * 512], BF16)
    RPF = sb(S1, "RPF", [128, 8 * 512], F32)
    RPO = sb(S1, "RPO", [128, 8 * 512], BF16)
    RPS = sb(S1, "RPS", [128, 1024], F32)
    EV1 = sb(S1, "EV1", [128, 512], BF16)
    EV2 = sb(S1, "EV2", [128, 512], BF16)
    S1a = open_stack()
    WDQs = sb(S1a, "WDQs", [128, DT * DCQ], BF16)
    WDKVs = sb(S1a, "WDKVs", [128, DT * DC], BF16)
    WKRs = sb(S1a, "WKRs", [128, DT * 1024], BF16)
    TBLG = sb(S1a, "TBLG", [128, 2 * 512], F32)

    for t in range(DT):
        dma("gpsimd", dG, WDQs[:, t * DCQ:(t + 1) * DCQ],
            wdq[t * 128:(t + 1) * 128, :])
        dma("gpsimd", dG, WDKVs[:, t * DC:(t + 1) * DC],
            wdkv[t * 128:(t + 1) * 128, :])
        dma("gpsimd", dG, WKRs[:, t * 1024:(t + 1) * 1024],
            wkr[t * 128:(t + 1) * 128, :])
    barrier()

    sT_group_end = 0
    prev_rope = 0
    for g in range(NG):
        c0, c1 = g * 512, (g + 1) * 512
        wait("sync", sT, sT_group_end)
        wait("sync", sV, prev_rope)
        ntb = dma("sync", dL, TBLG[:, 0:512], cosk[:, c0:c1])
        ntb = dma("sync", dL, TBLG[:, 512:1024], sink[:, c0:c1])
        nspill_last = None
        for i in range(4):
            r0 = g * 512 + i * 128
            px = i % 2
            nx = dma("sync", dL, XT[:, px * D:(px + 1) * D], xb[r0:r0 + 128, :])
            wait("vector", dL, nx)
            wait("scalar", dL, nx)
            nln = ln_tile(XT[:, px * D:(px + 1) * D], LNB[:, :])
            wait("scalar", sC, nln)
            if DEBUG and g == 0 and i == 0:
                dma("scalar", dS, dbg_out[:, 176:188], COL[:, 0:12])
            nspill_last = dma("scalar", dS, ln_hbm[r0:r0 + 128, :], LNB[:, :])
            wait("scalar", dS, nspill_last)
            wait("sync", sC, nln)
        wait("sync", dS, nspill_last)
        ntr = None
        for c in range(DT):
            ntr = dma("sync", dL, LNT[:, c * 512:(c + 1) * 512],
                      ln_hbm[c0:c1, c * 128:(c + 1) * 128], transpose=True)
        wait("tensor", dL, ntr)
        wait("vector", dL, ntb)

        if g < NGO and "qc" in P1A_PARTS:
            for i in range(KQ):
                pb = PSL[i % 2]
                def mmq(t, i=i, pb=pb):
                    r = None
                    for c in range(DT):
                        r = t.matmul(pb[:, :],
                                     WDQs[:, c * DCQ + i * 128:c * DCQ + (i + 1) * 128],
                                     LNT[:, c * 512:(c + 1) * 512],
                                     start=(c == 0), stop=(c == DT - 1))
                    return r
                nmm = run("tensor", mmq, sT)
                wait("vector", sT, nmm)
                nev = run("vector", lambda v, pb=pb: v.tensor_copy(EV1[:, :], pb[:, :]), sV)
                wait("tensor", sV, nev)
                wait("scalar", sV, nev)
                nd = dma("scalar", dS,
                         qcT_hbm[:, i * OWN + c0:i * OWN + c1], EV1[:, :])
                wait("vector", dS, nd)
        for i in range(KC if "c" in P1A_PARTS else 0):
            pb = PSL[2 + i % 2]
            def mmc(t, i=i, pb=pb):
                r = None
                for c in range(DT):
                    r = t.matmul(pb[:, :],
                                 WDKVs[:, c * DC + i * 128:c * DC + (i + 1) * 128],
                                 LNT[:, c * 512:(c + 1) * 512],
                                 start=(c == 0), stop=(c == DT - 1))
                return r
            nmm = run("tensor", mmc, sT)
            wait("vector", sT, nmm)
            nev = run("vector", lambda v, pb=pb: v.tensor_copy(EV2[:, :], pb[:, :]), sV)
            wait("tensor", sV, nev)
            wait("scalar", sV, nev)
            nd = dma("scalar", dS, cT_hbm[:, i * L + c0:i * L + c1], EV2[:, :])
            wait("vector", dS, nd)
        for i in range(8 if "kr" in P1A_PARTS else 0):
            pb = PSL[4 + i % 2]
            def mmk(t, i=i, pb=pb):
                r = None
                for c in range(DT):
                    r = t.matmul(pb[:, :],
                                 WKRs[:, c * 1024 + i * 128:c * 1024 + (i + 1) * 128],
                                 LNT[:, c * 512:(c + 1) * 512],
                                 start=(c == 0), stop=(c == DT - 1))
                return r
            nmm = run("tensor", mmk, sT)
            wait("vector", sT, nmm)
            nev = run("vector", lambda v, i=i, pb=pb: v.tensor_copy(
                RPF[:, i * 512:(i + 1) * 512], pb[:, :]), sV)
            wait("tensor", sV, nev)
        sT_group_end = sT.n
        for i in range(4 if "kr" in P1A_PARTS else 0):
            a = RPF[:, i * 512:(i + 1) * 512]
            b2 = RPF[:, (i + 4) * 512:(i + 5) * 512]
            oa = RPO[:, i * 512:(i + 1) * 512]
            ob2 = RPO[:, (i + 4) * 512:(i + 5) * 512]
            def rope(v, a=a, b2=b2, oa=oa, ob2=ob2):
                v.tensor_tensor(RPS[:, 0:512], a, TBLG[:, 0:512], OP.mult)
                v.tensor_tensor(RPS[:, 512:1024], b2, TBLG[:, 512:1024], OP.mult)
                v.tensor_tensor(oa, RPS[:, 0:512], RPS[:, 512:1024], OP.subtract)
                v.tensor_tensor(RPS[:, 0:512], b2, TBLG[:, 0:512], OP.mult)
                v.tensor_tensor(RPS[:, 512:1024], a, TBLG[:, 512:1024], OP.mult)
                return v.tensor_tensor(ob2, RPS[:, 0:512], RPS[:, 512:1024], OP.add)
            run("vector", rope)
        nro = run("vector", lambda v: v.nop(), sV)
        prev_rope = nro
        wait("scalar", sV, nro)
        ndl = None
        for i in range(4 if "kr" in P1A_PARTS else 0):
            for s in range(4):
                h = 4 * i + s
                ndl = dma("scalar", dS, k2_hbm[0:32, h * L + c0:h * L + c1],
                          RPO[s * 32:(s + 1) * 32, i * 512:(i + 1) * 512])
                ndl = dma("scalar", dS, k2_hbm[32:64, h * L + c0:h * L + c1],
                          RPO[s * 32:(s + 1) * 32, (i + 4) * 512:(i + 5) * 512])
        if ndl:
            wait("vector", dS, ndl)
    barrier()
    dbg_snap("ln", "qcT", "cT", "k2")
    S1a.close()
    if stop_after == "p1a":
        with root:
            blk = root.enter_context(nc.Block())
            for e in ("sync", "tensor", "vector", "scalar", "gpsimd"):
                def _mk(e):
                    def _f(eng):
                        for f in sched[e]:
                            f(eng)
                    return _f
                getattr(blk, e)(_mk(e))
            for s in reversed(phase_stacks):
                s.close()
        return nc

    S1b = open_stack()
    WUQs = sb(S1b, "WUQs", [128, KQ * D], BF16)
    WQRs = sb(S1b, "WQRs", [128, KQ * 1024], BF16)
    QCg = sb(S1b, "QCg", [128, KQ * 512], BF16)
    TBQG = sb(S1b, "TBQG", [128, 2 * 512], F32)
    for t in range(KQ):
        dma("gpsimd", dG, WUQs[:, t * D:(t + 1) * D], wuq[t * 128:(t + 1) * 128, :])
        dma("gpsimd", dG, WQRs[:, t * 1024:(t + 1) * 1024],
            wqr[t * 128:(t + 1) * 128, :])
    barrier()

    sT_group_end = 0
    prev_rope = 0
    for g in range(NGO):
        c0, c1 = g * 512, (g + 1) * 512
        wait("sync", sT, sT_group_end)
        wait("sync", sV, prev_rope)
        ntb = dma("sync", dL, TBQG[:, 0:512], cosq[:, c0:c1])
        ntb = dma("sync", dL, TBQG[:, 512:1024], sinq[:, c0:c1])
        in_ap = qcT_hbm[:, :].rearrange("p (i w) -> p i w", i=KQ)[:, :, c0:c1]
        out_ap = QCg[:, :].rearrange("p (i w) -> p i w", i=KQ)
        nq = dma("sync", dL, out_ap, in_ap)
        wait("tensor", dL, nq)
        wait("vector", dL, ntb)
        for h in range(NH):
            pb = PSL[h % 2]
            def mmqb(t, h=h, pb=pb):
                r = None
                for c in range(KQ):
                    r = t.matmul(pb[:, :],
                                 WUQs[:, c * D + h * 128:c * D + (h + 1) * 128],
                                 QCg[:, c * 512:(c + 1) * 512],
                                 start=(c == 0), stop=(c == KQ - 1))
                return r
            nmm = run("tensor", mmqb, sT)
            wait("vector", sT, nmm)
            nev = run("vector", lambda v, pb=pb: v.tensor_copy(EV1[:, :], pb[:, :]), sV)
            wait("tensor", sV, nev)
            wait("scalar", sV, nev)
            nd = dma("scalar", dS, qT_hbm[0:128, h * OWN + c0:h * OWN + c1],
                     EV1[:, :])
            wait("vector", dS, nd)
        for i in range(8):
            pb = PSL[2 + i % 2]
            def mmqr(t, i=i, pb=pb):
                r = None
                for c in range(KQ):
                    r = t.matmul(pb[:, :],
                                 WQRs[:, c * 1024 + i * 128:c * 1024 + (i + 1) * 128],
                                 QCg[:, c * 512:(c + 1) * 512],
                                 start=(c == 0), stop=(c == KQ - 1))
                return r
            nmm = run("tensor", mmqr, sT)
            wait("vector", sT, nmm)
            nev = run("vector", lambda v, i=i, pb=pb: v.tensor_copy(
                RPF[:, i * 512:(i + 1) * 512], pb[:, :]), sV)
            wait("tensor", sV, nev)
        sT_group_end = sT.n
        for i in range(4 if "kr" in P1A_PARTS else 0):
            a = RPF[:, i * 512:(i + 1) * 512]
            b2 = RPF[:, (i + 4) * 512:(i + 5) * 512]
            oa = RPO[:, i * 512:(i + 1) * 512]
            ob2 = RPO[:, (i + 4) * 512:(i + 5) * 512]
            def ropeq(v, a=a, b2=b2, oa=oa, ob2=ob2):
                v.tensor_tensor(RPS[:, 0:512], a, TBQG[:, 0:512], OP.mult)
                v.tensor_tensor(RPS[:, 512:1024], b2, TBQG[:, 512:1024], OP.mult)
                v.tensor_tensor(oa, RPS[:, 0:512], RPS[:, 512:1024], OP.subtract)
                v.tensor_tensor(RPS[:, 0:512], b2, TBQG[:, 0:512], OP.mult)
                v.tensor_tensor(RPS[:, 512:1024], a, TBQG[:, 512:1024], OP.mult)
                return v.tensor_tensor(ob2, RPS[:, 0:512], RPS[:, 512:1024], OP.add)
            run("vector", ropeq)
        nro = run("vector", lambda v: v.nop(), sV)
        prev_rope = nro
        wait("scalar", sV, nro)
        ndl = None
        for i in range(4):
            for s in range(4):
                h = 4 * i + s
                ndl = dma("scalar", dS,
                          qT_hbm[128:160, h * OWN + c0:h * OWN + c1],
                          RPO[s * 32:(s + 1) * 32, i * 512:(i + 1) * 512])
                ndl = dma("scalar", dS,
                          qT_hbm[160:192, h * OWN + c0:h * OWN + c1],
                          RPO[s * 32:(s + 1) * 32, (i + 4) * 512:(i + 5) * 512])
        wait("vector", dS, ndl)
    barrier()
    dbg_snap("qT")
    S1b.close()

    S1c = open_stack()
    WUKs = sb(S1c, "WUKs", [128, KC * D], BF16)
    WUVs = sb(S1c, "WUVs", [128, KC * D], BF16)
    CTg = sb(S1c, "CTg", [128, KC * 512], BF16)
    for t in range(KC):
        dma("gpsimd", dG, WUKs[:, t * D:(t + 1) * D], wuk[t * 128:(t + 1) * 128, :])
        dma("gpsimd", dG, WUVs[:, t * D:(t + 1) * D], wuv[t * 128:(t + 1) * 128, :])
    barrier()

    sT_group_end = 0
    for g in range(NG):
        c0, c1 = g * 512, (g + 1) * 512
        wait("sync", sT, sT_group_end)
        in_ap = cT_hbm[:, :].rearrange("p (i w) -> p i w", i=KC)[:, :, c0:c1]
        out_ap = CTg[:, :].rearrange("p (i w) -> p i w", i=KC)
        ncg = dma("sync", dL, out_ap, in_ap)
        wait("tensor", dL, ncg)
        for h in range(NH):
            pb = PSL[h % 2]
            def mmkb(t, h=h, pb=pb):
                r = None
                for c in range(KC):
                    r = t.matmul(pb[:, :],
                                 WUKs[:, c * D + h * 128:c * D + (h + 1) * 128],
                                 CTg[:, c * 512:(c + 1) * 512],
                                 start=(c == 0), stop=(c == KC - 1))
                return r
            nmm = run("tensor", mmkb, sT)
            wait("vector", sT, nmm)
            nev = run("vector", lambda v, pb=pb: v.tensor_copy(EV1[:, :], pb[:, :]), sV)
            wait("tensor", sV, nev)
            wait("scalar", sV, nev)
            nd = dma("scalar", dS, k1_hbm[:, h * L + c0:h * L + c1], EV1[:, :])
            wait("vector", dS, nd)
        for s in range(4):
            tt = 4 * g + s
            for dgrp in range(4):
                pb = PSL[2 + dgrp % 2]
                def mmv(t, s=s, dgrp=dgrp, pb=pb):
                    r = None
                    for c in range(KC):
                        r = t.matmul(pb[:, :],
                                     CTg[:, c * 512 + s * 128:c * 512 + (s + 1) * 128],
                                     WUVs[:, c * D + dgrp * 512:c * D + (dgrp + 1) * 512],
                                     start=(c == 0), stop=(c == KC - 1))
                    return r
                nmm = run("tensor", mmv, sT)
                wait("vector", sT, nmm)
                nev = run("vector", lambda v, pb=pb: v.tensor_copy(EV2[:, :], pb[:, :]), sV)
                wait("tensor", sV, nev)
                wait("scalar", sV, nev)
                nd = dma("scalar", dS,
                         v_hbm[:, tt * D + dgrp * 512:tt * D + (dgrp + 1) * 512],
                         EV2[:, :])
                wait("vector", dS, nd)
        sT_group_end = sT.n
    barrier()
    dbg_snap("k1", "v")
    S1c.close()
    S1.close()

    S2b = open_stack()
    ATT = sb(S2b, "ATT", [128, NH * OWN], BF16)
    S2a = open_stack()
    K1h = sb(S2a, "K1h", [128, L], BF16)
    K2h = sb(S2a, "K2h", [64, L], BF16)
    Vh = sb(S2a, "Vh", [128, L], BF16)
    QBh = sb(S2a, "QBh", [128, OWN], BF16)
    QRh = sb(S2a, "QRh", [64, OWN], BF16)
    PT = sb(S2a, "PT", [128, 3 * 512], BF16)
    DN = sb(S2a, "DN", [128, 4 * 512], F32)
    DNr = sb(S2a, "DNr", [1, 512], F32)
    RECB = sb(S2a, "RECB", [128, 512], F32)
    WMP = sb(S2a, "WMP", [128, D], F32)

    mp_list = []
    for mi, (w, nrow, ncol) in enumerate(
            ((gwT, D, HID), (vwT, D, HID), (owT, HID, D))):
        for r in range(nrow // 128):
            for cc2 in range(ncol // D):
                mp_list.append((mi, w, r, cc2))
    mp_idx = [0]
    mp_par = [0, 0, 0]

    def emit_meanpass(k):
        for _ in range(k):
            if mp_idx[0] >= len(mp_list):
                return
            mi, w, r, cc2 = mp_list[mp_idx[0]]
            mp_idx[0] += 1
            par = mp_par[mi]
            mp_par[mi] ^= 1
            nmp = dma("gpsimd", dG, WMP[:, :],
                      w[r * 128:(r + 1) * 128, cc2 * D:(cc2 + 1) * D])
            wait("vector", dG, nmp)
            def fred(v, mi=mi, par=par):
                v.reduce_sum(COL[:, 7:8], WMP[:, :], axis=AX.X,
                             apply_absolute_value=True)
                spacer(v)
                return v.tensor_tensor(WACC[:, 4 * (1 - par) + mi:4 * (1 - par) + mi + 1],
                                       WACC[:, 4 * par + mi:4 * par + mi + 1],
                                       COL[:, 7:8], OP.add)
            nred = run("vector", fred, sV)
            wait("gpsimd", sV, nred)

    sT_head_end = [0] * (NH + 1)
    nld = [None] * NH

    def load_head(h):
        n1 = dma("sync", dL, K1h[:, :], k1_hbm[:, h * L:(h + 1) * L])
        n1 = dma("sync", dL, K2h[:, :], k2_hbm[:, h * L:(h + 1) * L])
        vin = v_hbm[:, :].rearrange("p (k w) -> p k w", k=L // 128)[
            :, :, h * 128:(h + 1) * 128]
        vout = Vh[:, :].rearrange("p (k w) -> p k w", k=L // 128)
        n1 = dma("sync", dL, vout, vin)
        n1 = dma("sync", dL, QBh[:, :], qT_hbm[0:128, h * OWN:(h + 1) * OWN])
        n1 = dma("sync", dL, QRh[:, :], qT_hbm[128:192, h * OWN:(h + 1) * OWN])
        return n1

    for h in range(NH):
        wait("sync", sT, sT_head_end[h])
        nld[h] = load_head(h)
        wait("tensor", dL, nld[h])
        emit_meanpass(12)
        for qt in range(4):
            q0 = qt * 512
            pso = PSL[2 + qt]
            for kt in range(L // 128):
                psx = PSL[kt % 2]
                j = kt % 3
                def mms(t, kt=kt, q0=q0, psx=psx):
                    t.matmul(psx[:, :], K1h[:, kt * 128:(kt + 1) * 128],
                             QBh[:, q0:q0 + 512], start=True, stop=False)
                    return t.matmul(psx[:, :],
                                    K2h[0:64, kt * 128:(kt + 1) * 128],
                                    QRh[0:64, q0:q0 + 512],
                                    start=False, stop=True)
                nmm = run("tensor", mms, sT)
                wait("scalar", sT, nmm)
                nex = run("scalar", lambda s, j=j, psx=psx: s.activation(
                    PT[:, j * 512:(j + 1) * 512], psx[:, :], AF.Exp,
                    bias=CONST[:, 0:1], scale=SCALE), sC)
                wait("tensor", sC, nex)
                wait("vector", sC, nex)
                def fdn(v, j=j, qt=qt, first=(kt == 0)):
                    src = PT[:, j * 512:(j + 1) * 512]
                    d = DN[:, qt * 512:(qt + 1) * 512]
                    if first:
                        return v.tensor_copy(d, src)
                    return v.tensor_tensor(d, d, src, OP.add)
                ndn = run("vector", fdn, sV)
                def mav(t, kt=kt, j=j, pso=pso, first=(kt == 0)):
                    return t.matmul(pso[:, :], Vh[:, kt * 128:(kt + 1) * 128],
                                    PT[:, j * 512:(j + 1) * 512],
                                    start=first, stop=(kt == L // 128 - 1))
                nav = run("tensor", mav, sT)
                wait("scalar", sV, ndn)
                wait("scalar", sT, nav)
        for qt in range(4):
            psd = PSL[7]
            nds = run("tensor", lambda t, qt=qt, psd=psd: t.matmul(
                psd[0:1, :], CONST[:, 2:3], DN[:, qt * 512:(qt + 1) * 512],
                start=True, stop=True), sT)
            wait("vector", sT, nds)
            ndr = run("vector", lambda v, psd=psd: v.tensor_copy(
                DNr[:, :], psd[0:1, :]), sV)
            wait("tensor", sV, ndr)
            psb = PSL[6]
            nbc = run("tensor", lambda t, psb=psb: t.matmul(
                psb[:, :], ONE1[:, :], DNr[:, :], start=True, stop=True), sT)
            wait("vector", sT, nbc)
            pso = PSL[2 + qt]
            def fnorm(v, h=h, qt=qt, pso=pso, psb=psb):
                v.reciprocal(RECB[:, :], psb[:, :])
                return v.tensor_tensor(
                    ATT[:, h * OWN + qt * 512:h * OWN + (qt + 1) * 512],
                    pso[:, :], RECB[:, :], OP.mult)
            nnm = run("vector", fnorm, sV)
            wait("tensor", sV, nnm)
        sT_head_end[h + 1] = sT.n
    emit_meanpass(len(mp_list))
    barrier()

    cnts = [float(D * HID), float(D * HID), float(HID * D)]
    psg = PSL[0]
    ONESC = CONST[:, 2:3]
    for mi in range(3):
        par = mp_par[mi]
        run("tensor", lambda t, mi=mi, par=par: t.matmul(
            psg[0:1, 4 + mi:5 + mi], WACC[:, 4 * par + mi:4 * par + mi + 1],
            ONESC, start=True, stop=True, skip_group_check=True), sT)
    ngm = sT.n
    wait("vector", sT, ngm)
    def fgam(v):
        v.tensor_copy(COL[0:1, 4:7], psg[0:1, 4:7])
        spacer(v)
        for mi in range(3):
            v.tensor_scalar(COL[0:1, 8 + mi:9 + mi], COL[0:1, 4 + mi:5 + mi],
                            1.0 / cnts[mi], None, OP.mult)
        spacer(v)
        for mi in range(3):
            v.tensor_tensor(COL[0:1, mi:mi + 1], COL[0:1, 8 + mi:9 + mi],
                            CONST[0:1, 4:5], OP.max)
        return v.nop()
    nvg = run("vector", fgam, sV)
    wait("tensor", sV, nvg)
    psb = PSL[1]
    nb = run("tensor", lambda t, psb=psb: t.matmul(
        psb[:, 0:3], ONE1[:, :], COL[0:1, 0:3], start=True, stop=True), sT)
    wait("vector", sT, nb)
    def fgam2(v, psb=psb):
        v.tensor_scalar(GAM[:, 0:1], psb[:, 0:1], 1.0 / 127.0, None, OP.mult)
        v.tensor_scalar(GAM[:, 1:2], psb[:, 1:2], 1.0 / 127.0, None, OP.mult)
        v.tensor_scalar(GAM[:, 2:3], psb[:, 2:3], 1.0 / 127.0, None, OP.mult)
        v.tensor_scalar(GAM[:, 3:4], psb[:, 0:1], 0.5, None, OP.mult)
        v.tensor_scalar(GAM[:, 5:6], psb[:, 1:2], 0.5, None, OP.mult)
        v.tensor_scalar(GAM2[:, 0:1], psb[:, 2:3], 0.5, None, OP.mult)
        spacer(v)
        v.tensor_scalar(GAM[:, 4:5], GAM[:, 3:4], -1.0, None, OP.mult)
        v.tensor_scalar(GAM[:, 6:7], GAM[:, 5:6], -1.0, None, OP.mult)
        return v.tensor_scalar(GAM2[:, 1:2], GAM2[:, 0:1], -1.0, None, OP.mult)
    nv2 = run("vector", fgam2, sV)
    wait("tensor", sV, nv2)
    barrier()
    S2a.close()

    S3 = open_stack()
    WOs = sb(S3, "WOs", [128, NH * D], BF16)
    XR = sb(S3, "XR", [128, D], F32)
    RT = sb(S3, "RT", [128, D], F32)
    for t in range(NH):
        dma("gpsimd", dG, WOs[:, t * D:(t + 1) * D], wo[t * 128:(t + 1) * 128, :])
    barrier()

    for t in range(NT):
        nx = dma("sync", dL, XR[:, :], xb[t * 128:(t + 1) * 128, :])
        wait("vector", dL, nx)
        for dgrp in range(4):
            pb = PSL[dgrp % 2]
            def mmo(te, t=t, dgrp=dgrp, pb=pb):
                r = None
                for h in range(NH):
                    r = te.matmul(pb[:, :],
                                  ATT[:, h * OWN + t * 128:h * OWN + (t + 1) * 128],
                                  WOs[:, h * D + dgrp * 512:h * D + (dgrp + 1) * 512],
                                  start=(h == 0), stop=(h == NH - 1))
                return r
            nmm = run("tensor", mmo, sT)
            wait("vector", sT, nmm)
            nev = run("vector", lambda v, dgrp=dgrp, pb=pb: v.tensor_tensor(
                RT[:, dgrp * 512:(dgrp + 1) * 512], pb[:, :],
                XR[:, dgrp * 512:(dgrp + 1) * 512], OP.add), sV)
            wait("tensor", sV, nev)
        nrt = sV.n
        wait("scalar", sV, nrt)
        nd = dma("scalar", dS, r_hbm[t * 128:(t + 1) * 128, :], RT[:, :])
        wait("vector", dS, nd)
        wait("sync", dS, nd)
    barrier()
    dbg_snap("r")
    S3.close()
    S2b.close()

    S4b = open_stack()
    XQT = sb(S4b, "XQT", [128, DT * OWN], BF16)
    S4a = open_stack()
    LN2 = sb(S4a, "LN2", [128, D], F32)
    V0B = sb(S4a, "V0B", [128, D], F32)
    WKB = sb(S4a, "WKB", [128, D], F32)
    BIT = sb(S4a, "BIT", [128, D], F32)
    XQN = sb(S4a, "XQN", [128, D], BF16)
    XR4 = sb(S4a, "XR4", [128, D], F32)

    for t in range(NT):
        nr = dma("sync", dL, XR4[:, :], r_hbm[t * 128:(t + 1) * 128, :])
        wait("vector", dL, nr)
        wait("scalar", dL, nr)
        nln = ln_tile(XR4[:, :], LN2[:, :])
        wait("vector", sC, nln)
        def fgx(v, t=t):
            v.reduce_max(COL[:, 0:1], LN2[:, :], axis=AX.X,
                         apply_absolute_value=True)
            spacer(v)
            v.tensor_tensor(GX[:, t:t + 1], COL[:, 0:1], CONST[:, 4:5],
                            OP.max)
            spacer(v)
            return v.nop()
        run("vector", fgx)
        run("vector", recip_col(COL[:, 1:2], GX[:, t:t + 1], scale=127.0))
        def fq(v):
            spacer(v)
            v.tensor_scalar(V0B[:, :], LN2[:, :], COL[:, 1:2], 128.5,
                            OP.mult, OP.add)
            v.tensor_copy(WKB[:, :], V0B[:, :])
            for k in range(7, -1, -1):
                v.tensor_single_scalar(BIT[:, :], WKB[:, :], float(2 ** k), OP.is_ge)
                v.scalar_tensor_tensor(WKB[:, :], BIT[:, :], float(-(2 ** k)),
                                       WKB[:, :], OP.mult, OP.add)
            v.tensor_tensor(V0B[:, :], V0B[:, :], WKB[:, :], OP.subtract)
            return v.tensor_scalar(XQN[:, :], V0B[:, :], -128.0, None, OP.add)
        nq = run("vector", fq, sV)
        wait("scalar", sV, nq)
        nd = dma("scalar", dS, xq_hbm[t * 128:(t + 1) * 128, :], XQN[:, :])
        wait("vector", dS, nd)
        wait("sync", dS, nd)
    ntr = None
    for c in range(DT):
        ntr = dma("sync", dL, XQT[:, c * OWN:(c + 1) * OWN],
                  xq_hbm[:, c * 128:(c + 1) * 128], transpose=True)
    wait("tensor", dL, ntr)
    wait("vector", dL, ntr)
    if DEBUG:
        dma("sync", dL, dbg_out[:, 112:128], V0B[:, 0:16])
        dma("sync", dL, dbg_out[:, 128:144], WKB[:, 0:16])
        dma("sync", dL, dbg_out[:, 144:160], LN2[:, 0:16])
        dma("sync", dL, dbg_out[:, 160:162], COL[:, 0:2])
    barrier()
    dbg_snap("xq")
    S4a.close()

    S5 = open_stack()
    GWf = sb(S5, "GWf", [128, 2 * 512], F32)
    VWf = sb(S5, "VWf", [128, 2 * 512], F32)
    TGW = sb(S5, "TGW", [128, DT * 512], BF16)
    TVW = sb(S5, "TVW", [128, DT * 512], BF16)
    GBR = sb(S5, "GBR", [128, HID], BF16)
    VBR = sb(S5, "VBR", [128, HID], BF16)
    BRS = sb(S5, "BRS", [1, 512], F32)
    GT5 = sb(S5, "GT5", [128, 512], F32)
    VT5 = sb(S5, "VT5", [128, 512], F32)
    GV5 = sb(S5, "GV5", [128, 512], BF16)
    GS5 = sb(S5, "GS5", [128, 512], F32)

    for cc2 in range(HID // 512):
        nbr = dma("sync", dL, BRS[:, :], gb[0:1, cc2 * 512:(cc2 + 1) * 512])
        wait("tensor", dL, nbr)
        pb = PSL[cc2 % 2]
        nmm = run("tensor", lambda t, pb=pb: t.matmul(
            pb[:, :], ONE1[:, :], BRS[:, :], start=True, stop=True), sT)
        wait("vector", sT, nmm)
        nev = run("vector", lambda v, cc2=cc2, pb=pb: v.tensor_copy(
            GBR[:, cc2 * 512:(cc2 + 1) * 512], pb[:, :]), sV)
        wait("tensor", sV, nev)
        wait("sync", sV, nev)
    for cc2 in range(HID // 512):
        nbr = dma("sync", dL, BRS[:, :], vb[0:1, cc2 * 512:(cc2 + 1) * 512])
        wait("tensor", dL, nbr)
        pb = PSL[cc2 % 2]
        nmm = run("tensor", lambda t, pb=pb: t.matmul(
            pb[:, :], ONE1[:, :], BRS[:, :], start=True, stop=True), sT)
        wait("vector", sT, nmm)
        nev = run("vector", lambda v, cc2=cc2, pb=pb: v.tensor_copy(
            VBR[:, cc2 * 512:(cc2 + 1) * 512], pb[:, :]), sV)
        wait("tensor", sV, nev)
        wait("sync", sV, nev)
    for cc2 in range(D // 512):
        nbr = dma("sync", dL, BRS[:, :], ob[0:1, cc2 * 512:(cc2 + 1) * 512])
        wait("tensor", dL, nbr)
        pb = PSL[cc2 % 2]
        nmm = run("tensor", lambda t, pb=pb: t.matmul(
            pb[:, :], ONE1[:, :], BRS[:, :], start=True, stop=True), sT)
        wait("vector", sT, nmm)
        nev = run("vector", lambda v, cc2=cc2, pb=pb: v.tensor_copy(
            OUTB[:, cc2 * 512:(cc2 + 1) * 512], pb[:, :]), sV)
        wait("tensor", sV, nev)
        wait("sync", sV, nev)
    def fsg(v):
        for t in range(NT):
            v.tensor_tensor(SGc[:, t:t + 1], GX[:, t:t + 1], GAM[:, 0:1], OP.mult)
            v.tensor_tensor(SVc[:, t:t + 1], GX[:, t:t + 1], GAM[:, 1:2], OP.mult)
        return v.nop()
    run("vector", fsg)

    sT_chunk_end = 0
    for ch in range(HID // 512):
        wait("vector", sT, sT_chunk_end)
        nquant = None
        for c in range(DT):
            p = c % 2
            ng5 = dma("sync", dL, GWf[:, p * 512:(p + 1) * 512],
                      gwT[c * 128:(c + 1) * 128, ch * 512:(ch + 1) * 512])
            nv5 = dma("sync", dL, VWf[:, p * 512:(p + 1) * 512],
                      vwT[c * 128:(c + 1) * 128, ch * 512:(ch + 1) * 512])
            wait("vector", dL, nv5)
            def fquant(v, c=c, p=p):
                gsl = GWf[:, p * 512:(p + 1) * 512]
                v.tensor_scalar(GT5[:, :], gsl, GAM[:, 3:4], None, OP.is_ge)
                v.tensor_scalar(VT5[:, :], gsl, GAM[:, 4:5], None, OP.is_le)
                v.tensor_tensor(TGW[:, c * 512:(c + 1) * 512], GT5[:, :],
                                VT5[:, :], OP.subtract)
                vsl = VWf[:, p * 512:(p + 1) * 512]
                v.tensor_scalar(GT5[:, :], vsl, GAM[:, 5:6], None, OP.is_ge)
                v.tensor_scalar(VT5[:, :], vsl, GAM[:, 6:7], None, OP.is_le)
                return v.tensor_tensor(TVW[:, c * 512:(c + 1) * 512], GT5[:, :],
                                       VT5[:, :], OP.subtract)
            nquant = run("vector", fquant, sV)
            wait("sync", sV, nquant)
        wait("tensor", sV, nquant)
        for t in range(NT):
            pg = PSL[t % 2]
            pv = PSL[2 + t % 2]
            def mmg(te, t=t, pg=pg):
                r = None
                for c in range(DT):
                    r = te.matmul(pg[:, :],
                                  XQT[:, c * OWN + t * 128:c * OWN + (t + 1) * 128],
                                  TGW[:, c * 512:(c + 1) * 512],
                                  start=(c == 0), stop=(c == DT - 1))
                return r
            nmg = run("tensor", mmg, sT)
            def mmv2(te, t=t, pv=pv):
                r = None
                for c in range(DT):
                    r = te.matmul(pv[:, :],
                                  XQT[:, c * OWN + t * 128:c * OWN + (t + 1) * 128],
                                  TVW[:, c * 512:(c + 1) * 512],
                                  start=(c == 0), stop=(c == DT - 1))
                return r
            nmv = run("tensor", mmv2, sT)
            wait("vector", sT, nmg)
            nstg = run("vector", lambda v, t=t, ch=ch, pg=pg: v.scalar_tensor_tensor(
                GT5[:, :], pg[:, :], SGc[:, t:t + 1],
                GBR[:, ch * 512:(ch + 1) * 512], OP.mult, OP.add), sV)
            wait("scalar", sV, nstg)
            nsl = run("scalar", lambda s: s.activation(
                GS5[:, :], GT5[:, :], AF.Silu), sC)
            wait("vector", sT, nmv)
            nstv = run("vector", lambda v, t=t, ch=ch, pv=pv: v.scalar_tensor_tensor(
                VT5[:, :], pv[:, :], SVc[:, t:t + 1],
                VBR[:, ch * 512:(ch + 1) * 512], OP.mult, OP.add), sV)
            wait("tensor", sV, nstv)
            wait("vector", sC, nsl)
            par = ch % 2
            def fgv(v, t=t, par=par):
                v.tensor_tensor(GV5[:, :], GS5[:, :], VT5[:, :], OP.mult)
                v.reduce_max(COL[:, 0:1], GV5[:, :], axis=AX.X,
                             apply_absolute_value=True)
                spacer(v)
                src_m = GMX if par == 0 else GMXb
                dst_m = GMXb if par == 0 else GMX
                return v.tensor_tensor(dst_m[:, t:t + 1], src_m[:, t:t + 1],
                                       COL[:, 0:1], OP.max)
            ngv = run("vector", fgv, sV)
            wait("scalar", sV, ngv)
            nd = dma("scalar", dS,
                     gv_hbm[t * 128:(t + 1) * 128, ch * 512:(ch + 1) * 512],
                     GV5[:, :])
            wait("vector", dS, nd)
        sT_chunk_end = sT.n
    barrier()
    dbg_snap("gv")
    S5.close()
    S4b.close()

    S6 = open_stack()
    TOW = sb(S6, "TOW", [128, NHC * 1024], BF16)
    OWf6 = sb(S6, "OWf6", [128, 1024], F32)
    GQT = sb(S6, "GQT", [128, 2 * NHC * 128], BF16)
    SRr = sb(S6, "SRr", [1, 512], F32)
    GVT6 = sb(S6, "GVT6", [128, 2 * 512], BF16)
    U6 = sb(S6, "U6", [128, 512], F32)
    V06 = sb(S6, "V06", [128, 512], F32)
    W6 = sb(S6, "W6", [128, 512], F32)
    B6 = sb(S6, "B6", [128, 512], F32)
    GQ6 = sb(S6, "GQ6", [128, 512], BF16)
    YT = sb(S6, "YT", [128, 1024], F32)
    RT6 = sb(S6, "RT6", [128, 1024], F32)

    def fp6(v):
        for t in range(NT):
            v.tensor_tensor(GMX2[:, t:t + 1], GMX[:, t:t + 1], CONST[:, 4:5],
                            OP.max)
        spacer(v)
        for rep in range(4):
            v.tensor_copy(TRASH[:, rep * 16:(rep + 1) * 16], GMX2[:, :])
        spacer(v)
        v.reciprocal(TRASH[:, 64:128], TRASH[:, 0:64])
        spacer(v)
        v.tensor_scalar(SQc[:, :], TRASH[:, 64:80], 127.0, None, OP.mult)
        for t in range(NT):
            v.tensor_tensor(SOUTc[:, t:t + 1], GMX2[:, t:t + 1], GAM[:, 2:3],
                            OP.mult)
        return v.nop()
    np6 = run("vector", fp6, sV)
    wait("tensor", sV, np6)
    wait("scalar", sV, np6)

    for i in range(NHC):
        for colh in range(2):
            no = dma("sync", dL, OWf6[:, :],
                     owT[i * 128:(i + 1) * 128, colh * 1024:(colh + 1) * 1024])
            wait("vector", dL, no)
            def fqo(v, i=i, colh=colh):
                v.tensor_scalar(GT5[:, :], OWf6[:, 0:512], GAM2[:, 0:1], None,
                                OP.is_ge)
                v.tensor_scalar(VT5[:, :], OWf6[:, 0:512], GAM2[:, 1:2], None,
                                OP.is_le)
                dst1 = TOW[:, i * 1024:i * 1024 + 512] if colh == 0 else GV5[:, :]
                v.tensor_tensor(dst1, GT5[:, :], VT5[:, :], OP.subtract)
                v.tensor_scalar(GT5[:, :], OWf6[:, 512:1024], GAM2[:, 0:1], None,
                                OP.is_ge)
                v.tensor_scalar(VT5[:, :], OWf6[:, 512:1024], GAM2[:, 1:2], None,
                                OP.is_le)
                dst2 = (TOW[:, i * 1024 + 512:(i + 1) * 1024] if colh == 0
                        else GQ6[:, :])
                return v.tensor_tensor(dst2, GT5[:, :], VT5[:, :], OP.subtract)
            nq = run("vector", fqo, sV)
            wait("sync", sV, nq)
            if colh == 1:
                wait("scalar", sV, nq)
                nd = dma("scalar", dS, tow1_hbm[:, i * 1024:i * 1024 + 512],
                         GV5[:, :])
                nd = dma("scalar", dS,
                         tow1_hbm[:, i * 1024 + 512:(i + 1) * 1024], GQ6[:, :])
                wait("vector", dS, nd)

    nfu_end = [0, 0]
    for gq in range(4):
        psr = PSL[7]
        for s4 in range(4):
            t = gq * 4 + s4
            run("tensor", lambda te, t=t, s4=s4, psr=psr: te.matmul(
                psr[0:1, s4 * 128:(s4 + 1) * 128], SQc[:, t:t + 1], IDEN[:, :],
                start=(s4 == 0), stop=(s4 == 3)), sT)
        nsrow = sT.n
        wait("vector", sT, nsrow)
        nsr = run("vector", lambda v, psr=psr: v.tensor_copy(
            SRr[:, :], psr[0:1, :]), sV)
        wait("tensor", sV, nsr)
        psb6 = PSL[6]
        nbc = run("tensor", lambda t, psb6=psb6: t.matmul(
            psb6[:, :], ONE1[:, :], SRr[:, :], start=True, stop=True), sT)
        wait("vector", sT, nbc)
        for i in range(NHC):
            p = i % 2
            wait("sync", sV, nfu_end[p])
            ngt = dma("sync", dL, GVT6[:, p * 512:(p + 1) * 512],
                      gv_hbm[gq * 512:(gq + 1) * 512, i * 128:(i + 1) * 128],
                      transpose=True)
            wait("vector", dL, ngt)
            def fu(v, p=p, psb6=psb6):
                v.tensor_tensor(U6[:, :], GVT6[:, p * 512:(p + 1) * 512],
                                psb6[:, :], OP.mult)
                v.tensor_scalar(V06[:, :], U6[:, :], 128.5, None, OP.add)
                return v.tensor_copy(W6[:, :], V06[:, :])
            nfu = run("vector", fu, sV)
            nfu_end[p] = nfu
            wait("scalar", sV, nfu)
            for k in range(7, -1, -1):
                nsg2 = run("scalar", lambda s: s.activation(
                    B6[:, :], W6[:, :], AF.Sign, bias=CONST[:, 3:4],
                    scale=CONST[:, 2:3]), sC)
                wait("vector", sC, nsg2)
                coeff = -(2.0 ** (k - 1))
                npl = run("vector", lambda v, coeff=coeff: v.scalar_tensor_tensor(
                    W6[:, :], B6[:, :], coeff, W6[:, :], OP.mult, OP.add), sV)
                if k > 0:
                    wait("scalar", sV, npl)
            def ffin(v):
                v.tensor_tensor(V06[:, :], V06[:, :], W6[:, :], OP.subtract)
                return v.tensor_scalar(GQ6[:, :], V06[:, :], -0.5, None, OP.add)
            nfi = run("vector", ffin, sV)
            wait("scalar", sV, nfi)
            nd = dma("scalar", dS,
                     gvq_hbm[:, i * OWN + gq * 512:i * OWN + (gq + 1) * 512],
                     GQ6[:, :])
            wait("vector", dS, nd)
        wait("tensor", dS, dS.n)
    barrier()
    dbg_snap("gvq")

    for half in range(2):
        if half == 1:
            wait("sync", sT, sT.n)
            ntw = dma("sync", dL, TOW[:, :], tow1_hbm[:, :])
            wait("tensor", dL, ntw)
        ngq = [0, 0]
        sT_tile_end = [0] * (NT + 2)

        def load_gq(t):
            p = t % 2
            src = gvq_hbm[:, :].rearrange("p (i w) -> p i w", i=NHC)[
                :, :, t * 128:(t + 1) * 128]
            dstv = GQT[:, p * NHC * 128:(p + 1) * NHC * 128].rearrange(
                "p (i w) -> p i w", i=NHC)
            return dma("sync", dL, dstv, src)

        for t in range(NT):
            p = t % 2
            wait("sync", sT, sT_tile_end[t])
            ngq[p] = load_gq(t)
            nr = dma("sync", dL, RT6[:, :],
                     r_hbm[t * 128:(t + 1) * 128, half * 1024:(half + 1) * 1024])
            wait("tensor", dL, ngq[p])
            for dgrp in range(2):
                pb = PSL[dgrp]
                def mmo6(te, p=p, dgrp=dgrp, pb=pb):
                    r = None
                    for i in range(NHC):
                        r = te.matmul(
                            pb[:, :],
                            GQT[:, p * NHC * 128 + i * 128:p * NHC * 128 + (i + 1) * 128],
                            TOW[:, i * 1024 + dgrp * 512:i * 1024 + (dgrp + 1) * 512],
                            start=(i == 0), stop=(i == NHC - 1))
                    return r
                nmm = run("tensor", mmo6, sT)
                wait("vector", sT, nmm)
                wait("vector", dL, nr)
                def fep(v, t=t, dgrp=dgrp, half=half, pb=pb):
                    c6 = dgrp * 512
                    v.scalar_tensor_tensor(YT[:, c6:c6 + 512], pb[:, :],
                                           SOUTc[:, t:t + 1],
                                           OUTB[:, half * 1024 + c6:
                                                half * 1024 + c6 + 512],
                                           OP.mult, OP.add)
                    return v.tensor_tensor(YT[:, c6:c6 + 512], YT[:, c6:c6 + 512],
                                           RT6[:, c6:c6 + 512], OP.add)
                nep = run("vector", fep, sV)
                wait("tensor", sV, nep)
            sT_tile_end[t + 2] = sT.n
            nyv = sV.n
            wait("scalar", sV, nyv)
            nd = dma("scalar", dS,
                     y_out[t * 128:(t + 1) * 128, half * 1024:(half + 1) * 1024],
                     YT[:, :])
            wait("vector", dS, nd)
            wait("sync", dS, nd)
    barrier()
    if DEBUG:
        dma("sync", dL, dbg_out[:, 0:8], GAM[:, :])
        dma("sync", dL, dbg_out[:, 8:10], GAM2[:, :])
        dma("sync", dL, dbg_out[:, 200:208], WACC[:, :])
        dma("sync", dL, dbg_out[:, 16:32], GX[:, :])
        dma("sync", dL, dbg_out[:, 32:48], SGc[:, :])
        dma("sync", dL, dbg_out[:, 48:64], SVc[:, :])
        dma("sync", dL, dbg_out[:, 64:80], GMX[:, :])
        dma("sync", dL, dbg_out[:, 80:96], SQc[:, :])
        dma("sync", dL, dbg_out[:, 96:112], SOUTc[:, :])
        def fdone(eng):
            eng.wait_ge(dL.h, dL.n)
        emit("sync", fdone)
    barrier()

    with root:
        blk = root.enter_context(nc.Block())
        @blk.sync
        def _(eng):
            for f in sched["sync"]:
                f(eng)
        @blk.tensor
        def _(eng):
            for f in sched["tensor"]:
                f(eng)
        @blk.vector
        def _(eng):
            for f in sched["vector"]:
                f(eng)
        @blk.scalar
        def _(eng):
            for f in sched["scalar"]:
                f(eng)
        @blk.gpsimd
        def _(eng):
            for f in sched["gpsimd"]:
                f(eng)
        for s in reversed(phase_stacks):
            s.close()
    return nc


def _host_prep(inputs):
    x = np.ascontiguousarray(inputs["L0_attn_x"])
    f32 = np.float32
    shared = {
        "wdq": np.ascontiguousarray(inputs["L0_attn_W_D_Q"], dtype=f32),
        "wuq": np.ascontiguousarray(inputs["L0_attn_W_U_Q"], dtype=f32),
        "wdkv": np.ascontiguousarray(inputs["L0_attn_W_D_KV"], dtype=f32),
        "wuk": np.ascontiguousarray(inputs["L0_attn_W_U_K"], dtype=f32),
        "wuv": np.ascontiguousarray(inputs["L0_attn_W_U_V"], dtype=f32),
        "wo": np.ascontiguousarray(inputs["L0_attn_W_O"], dtype=f32),
        "gwT": np.ascontiguousarray(np.asarray(inputs["L0_ffn_gate_w"]).T),
        "vwT": np.ascontiguousarray(np.asarray(inputs["L0_ffn_val_w"]).T),
        "owT": np.ascontiguousarray(np.asarray(inputs["L0_ffn_out_w"]).T),
        "gb": np.ascontiguousarray(np.asarray(inputs["L0_ffn_gate_b"]).reshape(1, HID), dtype=f32),
        "vb": np.ascontiguousarray(np.asarray(inputs["L0_ffn_val_b"]).reshape(1, HID), dtype=f32),
        "ob": np.ascontiguousarray(np.asarray(inputs["L0_ffn_out_b"]).reshape(1, D), dtype=f32),
    }
    permA = (np.arange(NH)[:, None] * DHR + np.arange(32)[None, :]).reshape(-1)
    permB = (np.arange(NH)[:, None] * DHR + 32 + np.arange(32)[None, :]).reshape(-1)
    perm = np.concatenate([permA, permB])
    shared["wqr"] = np.ascontiguousarray(np.asarray(inputs["L0_attn_W_Q_R"])[:, perm], dtype=f32)
    shared["wkr"] = np.ascontiguousarray(np.asarray(inputs["L0_attn_W_K_R"])[:, perm], dtype=f32)

    inv_freq = 10000.0 ** (-(np.arange(0, DHR, 2, dtype=np.float64)) / DHR)
    pos = np.arange(L, dtype=np.float64)
    ang = pos[None, :] * inv_freq[:, None]
    cosk = np.tile(np.cos(ang), (4, 1)).astype(f32)
    sink = np.tile(np.sin(ang), (4, 1)).astype(f32)

    in_maps = []
    for core in range(8):
        b, qh = core // 2, core % 2
        m = dict(shared)
        xbv = np.asarray(x[b], dtype=f32)
        if qh == 1:
            xbv = np.concatenate([xbv[OWN:], xbv[:OWN]], axis=0)
            ck = np.concatenate([cosk[:, OWN:], cosk[:, :OWN]], axis=1)
            sk = np.concatenate([sink[:, OWN:], sink[:, :OWN]], axis=1)
        else:
            ck, sk = cosk, sink
        m["xb"] = np.ascontiguousarray(xbv)
        m["cosk"] = np.ascontiguousarray(ck)
        m["sink"] = np.ascontiguousarray(sk)
        m["cosq"] = np.ascontiguousarray(ck[:, 0:OWN])
        m["sinq"] = np.ascontiguousarray(sk[:, 0:OWN])
        in_maps.append(m)
    return in_maps


_NC_CACHE = {}


def kernel(**inputs):
    in_maps = _host_prep(inputs)
    if "nc" not in _NC_CACHE:
        _NC_CACHE["nc"] = build_program()
    nc = _NC_CACHE["nc"]
    res = run_bass_kernel_spmd(nc, in_maps, core_ids=list(range(8)))
    y = np.empty((B, L, D), dtype=np.float32)
    for core in range(8):
        b, qh = core // 2, core % 2
        y[b, qh * OWN:(qh + 1) * OWN, :] = np.asarray(res.results[core]["y"])
    return y


# version 03
# speedup vs baseline: 1.0007x; 1.0007x over previous
import contextlib
import numpy as np

import concourse.bass as bass
import concourse.mybir as mybir
from concourse.bass_utils import run_bass_kernel_spmd

F32 = mybir.dt.float32
BF16 = mybir.dt.bfloat16
F16 = mybir.dt.float16
AF = mybir.ActivationFunctionType
OP = mybir.AluOpType
AX = mybir.AxisListType

B, L, D = 4, 4096, 2048
NH, DH, DHR, DC, DCQ = 16, 128, 64, 512, 1536
HID = 8192
OWN = 2048
NG = L // 512
NGO = OWN // 512
NT = OWN // 128
DT = D // 128
KQ = DCQ // 128
KC = DC // 128
NHC = HID // 128
SCALE = DH ** -0.5
EXP_SHIFT = -12.0
EPS = 1e-5
N_ENG = 5
DEBUG = False
P1A_PARTS = ("qc", "c", "kr")


class Sem:
    def __init__(self, handle):
        self.h = handle
        self.n = 0

    def inc(self, k=1):
        self.n += k
        return self.n


def build_program(stop_after=None):
    nc = bass.Bass("TRN2", target_bir_lowering=False)
    sched = {e: [] for e in ("sync", "tensor", "vector", "scalar", "gpsimd")}

    def emit(e, fn):
        sched[e].append(fn)

    def inp(name, shape, dtype=F32):
        return nc.dram_tensor(name, shape, dtype, kind="ExternalInput")

    xb = inp("xb", [L, D])
    wdq = inp("wdq", [D, DCQ])
    wuq = inp("wuq", [DCQ, NH * DH])
    wqr = inp("wqr", [DCQ, NH * DHR])
    wdkv = inp("wdkv", [D, DC])
    wuk = inp("wuk", [DC, NH * DH])
    wkr = inp("wkr", [D, NH * DHR])
    wuv = inp("wuv", [DC, NH * DH])
    wo = inp("wo", [NH * DH, D])
    gwT = inp("gwT", [D, HID])
    vwT = inp("vwT", [D, HID])
    owT = inp("owT", [HID, D])
    gb = inp("gb", [1, HID])
    vb = inp("vb", [1, HID])
    ob = inp("ob", [1, D])
    cosk = inp("cosk", [128, L])
    sink = inp("sink", [128, L])
    cosq = inp("cosq", [128, OWN])
    sinq = inp("sinq", [128, OWN])

    y_out = nc.dram_tensor("y", [OWN, D], F32, kind="ExternalOutput")
    dbg_out = None
    dtw = {}
    if DEBUG:
        dbg_out = nc.dram_tensor("dbg", [128, 256], F32, kind="ExternalOutput")

    ln_hbm = nc.dram_tensor("ln_s", [L, D], F16, kind="Internal")
    qcT_hbm = nc.dram_tensor("qcT_s", [128, KQ * OWN], F16, kind="Internal")
    cT_hbm = nc.dram_tensor("cT_s", [128, KC * L], F16, kind="Internal")
    k1_hbm = nc.dram_tensor("k1_s", [128, NH * L], F16, kind="Internal")
    k2_hbm = nc.dram_tensor("k2_s", [64, NH * L], F16, kind="Internal")
    v_hbm = nc.dram_tensor("v_s", [128, (L // 128) * D], BF16, kind="Internal")
    qT_hbm = nc.dram_tensor("qT_s", [192, NH * OWN], F16, kind="Internal")
    r_hbm = nc.dram_tensor("r_s", [OWN, D], F32, kind="Internal")
    xq_hbm = nc.dram_tensor("xq_s", [OWN, D], F16, kind="Internal")
    gv_hbm = nc.dram_tensor("gv_s", [OWN, HID], F16, kind="Internal")
    gvq_hbm = nc.dram_tensor("gvq_s", [128, NHC * OWN], F16, kind="Internal")
    tow1_hbm = nc.dram_tensor("tow1_s", [128, NHC * 1024], F16, kind="Internal")

    if DEBUG:
        for nm, t in (("ln", ln_hbm), ("qcT", qcT_hbm), ("cT", cT_hbm),
                      ("k1", k1_hbm), ("k2", k2_hbm), ("v", v_hbm),
                      ("qT", qT_hbm), ("r", r_hbm), ("xq", xq_hbm),
                      ("gv", gv_hbm), ("gvq", gvq_hbm)):
            shp = [t.shape[0], t.shape[1]] if hasattr(t, "shape") else None
            dtw[nm] = (nc.dram_tensor("d_" + nm, list(t.shape),
                                      t.dtype, kind="ExternalOutput"), t)

    ones_row = nc.inline_tensor(np.ones((1, 128), dtype=np.float32), "ones_row_c")
    iden128 = nc.inline_tensor(np.eye(128, dtype=np.float32), "iden_c")

    root = contextlib.ExitStack()

    def mkstack():
        return root.enter_context(contextlib.ExitStack())

    phase_stacks = []

    def open_stack():
        s = contextlib.ExitStack()
        phase_stacks.append(s)
        return s

    def sb(stack, name, shape, dtype):
        return stack.enter_context(nc.sbuf_tensor(name, shape, dtype))

    S0 = open_stack()
    CONST = sb(S0, "CONST", [128, 8], F32)
    ONE1 = sb(S0, "ONE1", [1, 128], F32)
    IDEN = sb(S0, "IDEN", [128, 128], F32)
    RW8 = sb(S0, "RW8", [128, 8], F32)
    RW8b = sb(S0, "RW8b", [128, 8], F32)
    COL = sb(S0, "COL", [128, 12], F32)
    SPC = sb(S0, "SPC", [128, 4], F32)
    ONES8 = sb(S0, "ONES8", [128, 8], F32)
    GMXb = sb(S0, "GMXb", [128, NT], F32)
    GMX2 = sb(S0, "GMX2", [128, NT], F32)
    GX = sb(S0, "GX", [128, NT], F32)
    GMX = sb(S0, "GMX", [128, NT], F32)
    SGc = sb(S0, "SGc", [128, NT], F32)
    SVc = sb(S0, "SVc", [128, NT], F32)
    SQc = sb(S0, "SQc", [128, NT], F32)
    SOUTc = sb(S0, "SOUTc", [128, NT], F32)
    WACC = sb(S0, "WACC", [128, 8], F32)
    GAM = sb(S0, "GAM", [128, 8], F32)
    GAM2 = sb(S0, "GAM2", [128, 2], F32)
    TRASH = sb(S0, "TRASH", [128, 2048], F32)
    OUTB = sb(S0, "OUTB", [128, D], F32)

    PSL = [S0.enter_context(nc.psum_tensor(f"ps{i}", [128, 512], F32))
           for i in range(8)]

    def mksem(name):
        return Sem(S0.enter_context(nc.semaphore(name)))

    sT = mksem("sT")
    sV = mksem("sV")
    sC = mksem("sC")
    dL = mksem("dL")
    dS = mksem("dS")
    dG = mksem("dG")
    BAR = mksem("BAR")

    ph_no = [0]

    def barrier():
        ph_no[0] += 1
        tgt = N_ENG * ph_no[0]
        flush = {"sync": dL, "scalar": dS, "gpsimd": dG}
        for e in ("sync", "tensor", "vector", "scalar", "gpsimd"):
            fs = flush.get(e)
            fn_n = fs.n if fs else 0
            def f(eng, fs=fs, fn_n=fn_n, tgt=tgt):
                if fs is not None and fn_n > 0:
                    eng.wait_ge(fs.h, fn_n)
                eng.nop().then_inc(BAR.h, 1)
                eng.wait_ge(BAR.h, tgt)
            emit(e, f)
        BAR.inc(N_ENG)

    def dma(qe, qs, out_ap, in_ap, transpose=False):
        n = qs.inc(16)
        def f(eng, out_ap=out_ap, in_ap=in_ap, transpose=transpose):
            eng.dma_start(out_ap, in_ap, transpose=transpose).then_inc(qs.h, 16)
        emit(qe, f)
        return n

    def wait(e, sem, n):
        if n and n > 0:
            def f(eng, n=n, sem=sem):
                eng.wait_ge(sem.h, n)
            emit(e, f)

    def run(e, fn, sem=None):
        if sem is None:
            emit(e, fn)
            return None
        n = sem.inc()
        def f(eng, fn=fn, sem=sem):
            fn(eng).then_inc(sem.h, 1)
        emit(e, f)
        return n

    def dbg_snap(*names):
        if not DEBUG:
            return
        for nm in names:
            dst, srct = dtw[nm]
            dma("sync", dL, dst[:, :], srct[:, :])

    def spacer(v):
        v.tensor_copy(SPC[:, 0:4], CONST[:, 0:4])

    def recip_col(dst_col, src_col, scale=None):
        def f(v, dst=dst_col, src=src_col, scale=scale):
            v.tensor_scalar(RW8b[:, :], ONES8[:, :], src, None, OP.mult)
            spacer(v)
            v.reciprocal(RW8[:, :], RW8b[:, :])
            spacer(v)
            if scale is None:
                return v.tensor_copy(dst, RW8[:, 0:1])
            return v.tensor_scalar(dst, RW8[:, 0:1], float(scale), None, OP.mult)
        return f

    def ln_tile(src_ap, dst_ap, dst2_ap=None):
        wait("vector", sC, sC.n)
        wait("scalar", sV, sV.n)
        def fv(v):
            v.reduce_sum(COL[:, 0:1], src_ap, axis=AX.X)
            spacer(v)
            return v.tensor_scalar(COL[:, 2:3], COL[:, 0:1], 1.0 / D, None, OP.mult)
        run("vector", fv)
        nsq = run("scalar", lambda s: s.activation(
            TRASH[:, :], src_ap, AF.Square, accum_out=COL[:, 1:2]), sC)
        wait("vector", sC, nsq)
        def fv2(v):
            v.tensor_scalar(COL[:, 4:5], COL[:, 1:2], 1.0 / D, None, OP.mult)
            v.tensor_tensor(COL[:, 3:4], COL[:, 2:3], COL[:, 2:3], OP.mult)
            spacer(v)
            v.tensor_tensor(COL[:, 5:6], COL[:, 4:5], COL[:, 3:4], OP.subtract)
            spacer(v)
            return v.tensor_scalar(COL[:, 6:7], COL[:, 5:6], EPS, None, OP.add)
        run("vector", fv2)
        nv2 = run("vector", lambda v: v.nop(), sV)
        wait("scalar", sV, nv2)
        nst = run("scalar", lambda s: s.activation(
            COL[:, 8:9], COL[:, 6:7], AF.Sqrt), sC)
        wait("vector", sC, nst)
        run("vector", recip_col(COL[:, 9:10], COL[:, 8:9]))
        def fng(v):
            spacer(v)
            v.tensor_tensor(COL[:, 10:11], COL[:, 2:3], COL[:, 9:10], OP.mult)
            spacer(v)
            return v.tensor_scalar(COL[:, 11:12], COL[:, 10:11], -1.0, None, OP.mult)
        nng = run("vector", fng, sV)
        wait("scalar", sV, nng)
        nln = run("scalar", lambda s: s.activation(
            dst_ap, src_ap, AF.Identity, bias=COL[:, 11:12], scale=COL[:, 9:10]), sC)
        if dst2_ap is not None:
            nln = run("scalar", lambda s: s.activation(
                dst2_ap, src_ap, AF.Identity,
                bias=COL[:, 11:12], scale=COL[:, 9:10]), sC)
        return nln

    def g0(g):
        g.memset(CONST[:, 0:1], EXP_SHIFT)
        g.memset(CONST[:, 1:2], 0.0)
        g.memset(CONST[:, 2:3], 1.0)
        g.memset(CONST[:, 3:4], -128.0)
        g.memset(CONST[:, 4:5], 1e-5)
        g.memset(WACC[:, :], 0.0)
        g.memset(GMX[:, :], 0.0)
        g.memset(GMXb[:, :], 0.0)
        g.memset(ONES8[:, :], 1.0)
        return g.nop()
    run("gpsimd", g0)
    dma("sync", dL, ONE1[:, :], ones_row[:, :])
    dma("sync", dL, IDEN[:, :], iden128[:, :])

    S1 = open_stack()
    XT = sb(S1, "XT", [128, 2 * D], F32)
    LNB = sb(S1, "LNB", [128, D], F16)
    LNT = sb(S1, "LNT", [128, DT * 512], F16)
    RPF = sb(S1, "RPF", [128, 8 * 512], F32)
    RPO = sb(S1, "RPO", [128, 8 * 512], F16)
    RPS = sb(S1, "RPS", [128, 1024], F32)
    EV1 = sb(S1, "EV1", [128, 512], F16)
    EV2 = sb(S1, "EV2", [128, 512], F16)
    EVV = sb(S1, "EVV", [128, 512], BF16)
    S1a = open_stack()
    WDQs = sb(S1a, "WDQs", [128, DT * DCQ], F16)
    WDKVs = sb(S1a, "WDKVs", [128, DT * DC], F16)
    WKRs = sb(S1a, "WKRs", [128, DT * 1024], F16)
    TBLG = sb(S1a, "TBLG", [128, 2 * 512], F32)

    for t in range(DT):
        dma("gpsimd", dG, WDQs[:, t * DCQ:(t + 1) * DCQ],
            wdq[t * 128:(t + 1) * 128, :])
        dma("gpsimd", dG, WDKVs[:, t * DC:(t + 1) * DC],
            wdkv[t * 128:(t + 1) * 128, :])
        dma("gpsimd", dG, WKRs[:, t * 1024:(t + 1) * 1024],
            wkr[t * 128:(t + 1) * 128, :])
    barrier()

    sT_group_end = 0
    prev_rope = 0
    for g in range(NG):
        c0, c1 = g * 512, (g + 1) * 512
        wait("sync", sT, sT_group_end)
        wait("sync", sV, prev_rope)
        ntb = dma("sync", dL, TBLG[:, 0:512], cosk[:, c0:c1])
        ntb = dma("sync", dL, TBLG[:, 512:1024], sink[:, c0:c1])
        nspill_last = None
        for i in range(4):
            r0 = g * 512 + i * 128
            px = i % 2
            nx = dma("sync", dL, XT[:, px * D:(px + 1) * D], xb[r0:r0 + 128, :])
            wait("vector", dL, nx)
            wait("scalar", dL, nx)
            nln = ln_tile(XT[:, px * D:(px + 1) * D], LNB[:, :])
            wait("scalar", sC, nln)
            if DEBUG and g == 0 and i == 0:
                dma("scalar", dS, dbg_out[:, 176:188], COL[:, 0:12])
            nspill_last = dma("scalar", dS, ln_hbm[r0:r0 + 128, :], LNB[:, :])
            wait("scalar", dS, nspill_last)
            wait("sync", sC, nln)
        wait("sync", dS, nspill_last)
        ntr = None
        for c in range(DT):
            ntr = dma("sync", dL, LNT[:, c * 512:(c + 1) * 512],
                      ln_hbm[c0:c1, c * 128:(c + 1) * 128], transpose=True)
        wait("tensor", dL, ntr)
        wait("vector", dL, ntb)

        if g < NGO and "qc" in P1A_PARTS:
            for i in range(KQ):
                pb = PSL[i % 2]
                def mmq(t, i=i, pb=pb):
                    r = None
                    for c in range(DT):
                        r = t.matmul(pb[:, :],
                                     WDQs[:, c * DCQ + i * 128:c * DCQ + (i + 1) * 128],
                                     LNT[:, c * 512:(c + 1) * 512],
                                     start=(c == 0), stop=(c == DT - 1))
                    return r
                nmm = run("tensor", mmq, sT)
                wait("vector", sT, nmm)
                nev = run("vector", lambda v, pb=pb: v.tensor_copy(EV1[:, :], pb[:, :]), sV)
                wait("tensor", sV, nev)
                wait("scalar", sV, nev)
                nd = dma("scalar", dS,
                         qcT_hbm[:, i * OWN + c0:i * OWN + c1], EV1[:, :])
                wait("vector", dS, nd)
        for i in range(KC if "c" in P1A_PARTS else 0):
            pb = PSL[2 + i % 2]
            def mmc(t, i=i, pb=pb):
                r = None
                for c in range(DT):
                    r = t.matmul(pb[:, :],
                                 WDKVs[:, c * DC + i * 128:c * DC + (i + 1) * 128],
                                 LNT[:, c * 512:(c + 1) * 512],
                                 start=(c == 0), stop=(c == DT - 1))
                return r
            nmm = run("tensor", mmc, sT)
            wait("vector", sT, nmm)
            nev = run("vector", lambda v, pb=pb: v.tensor_copy(EV2[:, :], pb[:, :]), sV)
            wait("tensor", sV, nev)
            wait("scalar", sV, nev)
            nd = dma("scalar", dS, cT_hbm[:, i * L + c0:i * L + c1], EV2[:, :])
            wait("vector", dS, nd)
        for i in range(8 if "kr" in P1A_PARTS else 0):
            pb = PSL[4 + i % 2]
            def mmk(t, i=i, pb=pb):
                r = None
                for c in range(DT):
                    r = t.matmul(pb[:, :],
                                 WKRs[:, c * 1024 + i * 128:c * 1024 + (i + 1) * 128],
                                 LNT[:, c * 512:(c + 1) * 512],
                                 start=(c == 0), stop=(c == DT - 1))
                return r
            nmm = run("tensor", mmk, sT)
            wait("vector", sT, nmm)
            nev = run("vector", lambda v, i=i, pb=pb: v.tensor_copy(
                RPF[:, i * 512:(i + 1) * 512], pb[:, :]), sV)
            wait("tensor", sV, nev)
        sT_group_end = sT.n
        for i in range(4 if "kr" in P1A_PARTS else 0):
            a = RPF[:, i * 512:(i + 1) * 512]
            b2 = RPF[:, (i + 4) * 512:(i + 5) * 512]
            oa = RPO[:, i * 512:(i + 1) * 512]
            ob2 = RPO[:, (i + 4) * 512:(i + 5) * 512]
            def rope(v, a=a, b2=b2, oa=oa, ob2=ob2):
                v.tensor_tensor(RPS[:, 0:512], a, TBLG[:, 0:512], OP.mult)
                v.tensor_tensor(RPS[:, 512:1024], b2, TBLG[:, 512:1024], OP.mult)
                v.tensor_tensor(oa, RPS[:, 0:512], RPS[:, 512:1024], OP.subtract)
                v.tensor_tensor(RPS[:, 0:512], b2, TBLG[:, 0:512], OP.mult)
                v.tensor_tensor(RPS[:, 512:1024], a, TBLG[:, 512:1024], OP.mult)
                return v.tensor_tensor(ob2, RPS[:, 0:512], RPS[:, 512:1024], OP.add)
            run("vector", rope)
        nro = run("vector", lambda v: v.nop(), sV)
        prev_rope = nro
        wait("scalar", sV, nro)
        ndl = None
        for i in range(4 if "kr" in P1A_PARTS else 0):
            for s in range(4):
                h = 4 * i + s
                ndl = dma("scalar", dS, k2_hbm[0:32, h * L + c0:h * L + c1],
                          RPO[s * 32:(s + 1) * 32, i * 512:(i + 1) * 512])
                ndl = dma("scalar", dS, k2_hbm[32:64, h * L + c0:h * L + c1],
                          RPO[s * 32:(s + 1) * 32, (i + 4) * 512:(i + 5) * 512])
        if ndl:
            wait("vector", dS, ndl)
    barrier()
    dbg_snap("ln", "qcT", "cT", "k2")
    S1a.close()
    if stop_after == "p1a":
        with root:
            blk = root.enter_context(nc.Block())
            for e in ("sync", "tensor", "vector", "scalar", "gpsimd"):
                def _mk(e):
                    def _f(eng):
                        for f in sched[e]:
                            f(eng)
                    return _f
                getattr(blk, e)(_mk(e))
            for s in reversed(phase_stacks):
                s.close()
        return nc

    S1b = open_stack()
    WUQs = sb(S1b, "WUQs", [128, KQ * D], F16)
    WQRs = sb(S1b, "WQRs", [128, KQ * 1024], F16)
    QCg = sb(S1b, "QCg", [128, KQ * 512], F16)
    TBQG = sb(S1b, "TBQG", [128, 2 * 512], F32)
    for t in range(KQ):
        dma("gpsimd", dG, WUQs[:, t * D:(t + 1) * D], wuq[t * 128:(t + 1) * 128, :])
        dma("gpsimd", dG, WQRs[:, t * 1024:(t + 1) * 1024],
            wqr[t * 128:(t + 1) * 128, :])
    barrier()

    sT_group_end = 0
    prev_rope = 0
    for g in range(NGO):
        c0, c1 = g * 512, (g + 1) * 512
        wait("sync", sT, sT_group_end)
        wait("sync", sV, prev_rope)
        ntb = dma("sync", dL, TBQG[:, 0:512], cosq[:, c0:c1])
        ntb = dma("sync", dL, TBQG[:, 512:1024], sinq[:, c0:c1])
        in_ap = qcT_hbm[:, :].rearrange("p (i w) -> p i w", i=KQ)[:, :, c0:c1]
        out_ap = QCg[:, :].rearrange("p (i w) -> p i w", i=KQ)
        nq = dma("sync", dL, out_ap, in_ap)
        wait("tensor", dL, nq)
        wait("vector", dL, ntb)
        for h in range(NH):
            pb = PSL[h % 2]
            def mmqb(t, h=h, pb=pb):
                r = None
                for c in range(KQ):
                    r = t.matmul(pb[:, :],
                                 WUQs[:, c * D + h * 128:c * D + (h + 1) * 128],
                                 QCg[:, c * 512:(c + 1) * 512],
                                 start=(c == 0), stop=(c == KQ - 1))
                return r
            nmm = run("tensor", mmqb, sT)
            wait("vector", sT, nmm)
            nev = run("vector", lambda v, pb=pb: v.tensor_copy(EV1[:, :], pb[:, :]), sV)
            wait("tensor", sV, nev)
            wait("scalar", sV, nev)
            nd = dma("scalar", dS, qT_hbm[0:128, h * OWN + c0:h * OWN + c1],
                     EV1[:, :])
            wait("vector", dS, nd)
        for i in range(8):
            pb = PSL[2 + i % 2]
            def mmqr(t, i=i, pb=pb):
                r = None
                for c in range(KQ):
                    r = t.matmul(pb[:, :],
                                 WQRs[:, c * 1024 + i * 128:c * 1024 + (i + 1) * 128],
                                 QCg[:, c * 512:(c + 1) * 512],
                                 start=(c == 0), stop=(c == KQ - 1))
                return r
            nmm = run("tensor", mmqr, sT)
            wait("vector", sT, nmm)
            nev = run("vector", lambda v, i=i, pb=pb: v.tensor_copy(
                RPF[:, i * 512:(i + 1) * 512], pb[:, :]), sV)
            wait("tensor", sV, nev)
        sT_group_end = sT.n
        for i in range(4 if "kr" in P1A_PARTS else 0):
            a = RPF[:, i * 512:(i + 1) * 512]
            b2 = RPF[:, (i + 4) * 512:(i + 5) * 512]
            oa = RPO[:, i * 512:(i + 1) * 512]
            ob2 = RPO[:, (i + 4) * 512:(i + 5) * 512]
            def ropeq(v, a=a, b2=b2, oa=oa, ob2=ob2):
                v.tensor_tensor(RPS[:, 0:512], a, TBQG[:, 0:512], OP.mult)
                v.tensor_tensor(RPS[:, 512:1024], b2, TBQG[:, 512:1024], OP.mult)
                v.tensor_tensor(oa, RPS[:, 0:512], RPS[:, 512:1024], OP.subtract)
                v.tensor_tensor(RPS[:, 0:512], b2, TBQG[:, 0:512], OP.mult)
                v.tensor_tensor(RPS[:, 512:1024], a, TBQG[:, 512:1024], OP.mult)
                return v.tensor_tensor(ob2, RPS[:, 0:512], RPS[:, 512:1024], OP.add)
            run("vector", ropeq)
        nro = run("vector", lambda v: v.nop(), sV)
        prev_rope = nro
        wait("scalar", sV, nro)
        ndl = None
        for i in range(4):
            for s in range(4):
                h = 4 * i + s
                ndl = dma("scalar", dS,
                          qT_hbm[128:160, h * OWN + c0:h * OWN + c1],
                          RPO[s * 32:(s + 1) * 32, i * 512:(i + 1) * 512])
                ndl = dma("scalar", dS,
                          qT_hbm[160:192, h * OWN + c0:h * OWN + c1],
                          RPO[s * 32:(s + 1) * 32, (i + 4) * 512:(i + 5) * 512])
        wait("vector", dS, ndl)
    barrier()
    dbg_snap("qT")
    S1b.close()

    S1c = open_stack()
    WUKs = sb(S1c, "WUKs", [128, KC * D], F16)
    WUVs = sb(S1c, "WUVs", [128, KC * D], F16)
    CTg = sb(S1c, "CTg", [128, KC * 512], F16)
    for t in range(KC):
        dma("gpsimd", dG, WUKs[:, t * D:(t + 1) * D], wuk[t * 128:(t + 1) * 128, :])
        dma("gpsimd", dG, WUVs[:, t * D:(t + 1) * D], wuv[t * 128:(t + 1) * 128, :])
    barrier()

    sT_group_end = 0
    for g in range(NG):
        c0, c1 = g * 512, (g + 1) * 512
        wait("sync", sT, sT_group_end)
        in_ap = cT_hbm[:, :].rearrange("p (i w) -> p i w", i=KC)[:, :, c0:c1]
        out_ap = CTg[:, :].rearrange("p (i w) -> p i w", i=KC)
        ncg = dma("sync", dL, out_ap, in_ap)
        wait("tensor", dL, ncg)
        for h in range(NH):
            pb = PSL[h % 2]
            def mmkb(t, h=h, pb=pb):
                r = None
                for c in range(KC):
                    r = t.matmul(pb[:, :],
                                 WUKs[:, c * D + h * 128:c * D + (h + 1) * 128],
                                 CTg[:, c * 512:(c + 1) * 512],
                                 start=(c == 0), stop=(c == KC - 1))
                return r
            nmm = run("tensor", mmkb, sT)
            wait("vector", sT, nmm)
            nev = run("vector", lambda v, pb=pb: v.tensor_copy(EV1[:, :], pb[:, :]), sV)
            wait("tensor", sV, nev)
            wait("scalar", sV, nev)
            nd = dma("scalar", dS, k1_hbm[:, h * L + c0:h * L + c1], EV1[:, :])
            wait("vector", dS, nd)
        for s in range(4):
            tt = 4 * g + s
            for dgrp in range(4):
                pb = PSL[2 + dgrp % 2]
                def mmv(t, s=s, dgrp=dgrp, pb=pb):
                    r = None
                    for c in range(KC):
                        r = t.matmul(pb[:, :],
                                     CTg[:, c * 512 + s * 128:c * 512 + (s + 1) * 128],
                                     WUVs[:, c * D + dgrp * 512:c * D + (dgrp + 1) * 512],
                                     start=(c == 0), stop=(c == KC - 1))
                    return r
                nmm = run("tensor", mmv, sT)
                wait("vector", sT, nmm)
                nev = run("vector", lambda v, pb=pb: v.tensor_copy(EVV[:, :], pb[:, :]), sV)
                wait("tensor", sV, nev)
                wait("scalar", sV, nev)
                nd = dma("scalar", dS,
                         v_hbm[:, tt * D + dgrp * 512:tt * D + (dgrp + 1) * 512],
                         EVV[:, :])
                wait("vector", dS, nd)
        sT_group_end = sT.n
    barrier()
    dbg_snap("k1", "v")
    S1c.close()
    S1.close()

    S2b = open_stack()
    ATT = sb(S2b, "ATT", [128, NH * OWN], F16)
    S2a = open_stack()
    K1h = sb(S2a, "K1h", [128, L], F16)
    K2h = sb(S2a, "K2h", [64, L], F16)
    Vh = sb(S2a, "Vh", [128, L], BF16)
    QBh = sb(S2a, "QBh", [128, OWN], F16)
    QRh = sb(S2a, "QRh", [64, OWN], F16)
    PT = sb(S2a, "PT", [128, 3 * 512], BF16)
    DN = sb(S2a, "DN", [128, 4 * 512], F32)
    DNr = sb(S2a, "DNr", [1, 512], F32)
    RECB = sb(S2a, "RECB", [128, 512], F32)
    WMP = sb(S2a, "WMP", [128, D], F32)

    mp_list = []
    for mi, (w, nrow, ncol) in enumerate(
            ((gwT, D, HID), (vwT, D, HID), (owT, HID, D))):
        for r in range(nrow // 128):
            for cc2 in range(ncol // D):
                mp_list.append((mi, w, r, cc2))
    mp_idx = [0]
    mp_par = [0, 0, 0]

    def emit_meanpass(k):
        for _ in range(k):
            if mp_idx[0] >= len(mp_list):
                return
            mi, w, r, cc2 = mp_list[mp_idx[0]]
            mp_idx[0] += 1
            par = mp_par[mi]
            mp_par[mi] ^= 1
            nmp = dma("gpsimd", dG, WMP[:, :],
                      w[r * 128:(r + 1) * 128, cc2 * D:(cc2 + 1) * D])
            wait("vector", dG, nmp)
            def fred(v, mi=mi, par=par):
                v.reduce_sum(COL[:, 7:8], WMP[:, :], axis=AX.X,
                             apply_absolute_value=True)
                spacer(v)
                return v.tensor_tensor(WACC[:, 4 * (1 - par) + mi:4 * (1 - par) + mi + 1],
                                       WACC[:, 4 * par + mi:4 * par + mi + 1],
                                       COL[:, 7:8], OP.add)
            nred = run("vector", fred, sV)
            wait("gpsimd", sV, nred)

    sT_head_end = [0] * (NH + 1)
    nld = [None] * NH

    def load_head(h):
        n1 = dma("sync", dL, K1h[:, :], k1_hbm[:, h * L:(h + 1) * L])
        n1 = dma("sync", dL, K2h[:, :], k2_hbm[:, h * L:(h + 1) * L])
        vin = v_hbm[:, :].rearrange("p (k w) -> p k w", k=L // 128)[
            :, :, h * 128:(h + 1) * 128]
        vout = Vh[:, :].rearrange("p (k w) -> p k w", k=L // 128)
        n1 = dma("sync", dL, vout, vin)
        n1 = dma("sync", dL, QBh[:, :], qT_hbm[0:128, h * OWN:(h + 1) * OWN])
        n1 = dma("sync", dL, QRh[:, :], qT_hbm[128:192, h * OWN:(h + 1) * OWN])
        return n1

    for h in range(NH):
        wait("sync", sT, sT_head_end[h])
        nld[h] = load_head(h)
        wait("tensor", dL, nld[h])
        emit_meanpass(12)
        for qt in range(4):
            q0 = qt * 512
            pso = PSL[2 + qt]
            for kt in range(L // 128):
                psx = PSL[kt % 2]
                j = kt % 3
                def mms(t, kt=kt, q0=q0, psx=psx):
                    t.matmul(psx[:, :], K1h[:, kt * 128:(kt + 1) * 128],
                             QBh[:, q0:q0 + 512], start=True, stop=False)
                    return t.matmul(psx[:, :],
                                    K2h[0:64, kt * 128:(kt + 1) * 128],
                                    QRh[0:64, q0:q0 + 512],
                                    start=False, stop=True)
                nmm = run("tensor", mms, sT)
                wait("scalar", sT, nmm)
                nex = run("scalar", lambda s, j=j, psx=psx: s.activation(
                    PT[:, j * 512:(j + 1) * 512], psx[:, :], AF.Exp,
                    bias=CONST[:, 0:1], scale=SCALE), sC)
                wait("tensor", sC, nex)
                wait("vector", sC, nex)
                def fdn(v, j=j, qt=qt, first=(kt == 0)):
                    src = PT[:, j * 512:(j + 1) * 512]
                    d = DN[:, qt * 512:(qt + 1) * 512]
                    if first:
                        return v.tensor_copy(d, src)
                    return v.tensor_tensor(d, d, src, OP.add)
                ndn = run("vector", fdn, sV)
                def mav(t, kt=kt, j=j, pso=pso, first=(kt == 0)):
                    return t.matmul(pso[:, :], Vh[:, kt * 128:(kt + 1) * 128],
                                    PT[:, j * 512:(j + 1) * 512],
                                    start=first, stop=(kt == L // 128 - 1))
                nav = run("tensor", mav, sT)
                wait("scalar", sV, ndn)
                wait("scalar", sT, nav)
        for qt in range(4):
            psd = PSL[7]
            nds = run("tensor", lambda t, qt=qt, psd=psd: t.matmul(
                psd[0:1, :], CONST[:, 2:3], DN[:, qt * 512:(qt + 1) * 512],
                start=True, stop=True), sT)
            wait("vector", sT, nds)
            ndr = run("vector", lambda v, psd=psd: v.tensor_copy(
                DNr[:, :], psd[0:1, :]), sV)
            wait("tensor", sV, ndr)
            psb = PSL[6]
            nbc = run("tensor", lambda t, psb=psb: t.matmul(
                psb[:, :], ONE1[:, :], DNr[:, :], start=True, stop=True), sT)
            wait("vector", sT, nbc)
            pso = PSL[2 + qt]
            def fnorm(v, h=h, qt=qt, pso=pso, psb=psb):
                v.reciprocal(RECB[:, :], psb[:, :])
                return v.tensor_tensor(
                    ATT[:, h * OWN + qt * 512:h * OWN + (qt + 1) * 512],
                    pso[:, :], RECB[:, :], OP.mult)
            nnm = run("vector", fnorm, sV)
            wait("tensor", sV, nnm)
        sT_head_end[h + 1] = sT.n
    emit_meanpass(len(mp_list))
    barrier()

    cnts = [float(D * HID), float(D * HID), float(HID * D)]
    psg = PSL[0]
    ONESC = CONST[:, 2:3]
    for mi in range(3):
        par = mp_par[mi]
        run("tensor", lambda t, mi=mi, par=par: t.matmul(
            psg[0:1, 4 + mi:5 + mi], WACC[:, 4 * par + mi:4 * par + mi + 1],
            ONESC, start=True, stop=True, skip_group_check=True), sT)
    ngm = sT.n
    wait("vector", sT, ngm)
    def fgam(v):
        v.tensor_copy(COL[0:1, 4:7], psg[0:1, 4:7])
        spacer(v)
        for mi in range(3):
            v.tensor_scalar(COL[0:1, 8 + mi:9 + mi], COL[0:1, 4 + mi:5 + mi],
                            1.0 / cnts[mi], None, OP.mult)
        spacer(v)
        for mi in range(3):
            v.tensor_tensor(COL[0:1, mi:mi + 1], COL[0:1, 8 + mi:9 + mi],
                            CONST[0:1, 4:5], OP.max)
        return v.nop()
    nvg = run("vector", fgam, sV)
    wait("tensor", sV, nvg)
    psb = PSL[1]
    nb = run("tensor", lambda t, psb=psb: t.matmul(
        psb[:, 0:3], ONE1[:, :], COL[0:1, 0:3], start=True, stop=True), sT)
    wait("vector", sT, nb)
    def fgam2(v, psb=psb):
        v.tensor_scalar(GAM[:, 0:1], psb[:, 0:1], 1.0 / 127.0, None, OP.mult)
        v.tensor_scalar(GAM[:, 1:2], psb[:, 1:2], 1.0 / 127.0, None, OP.mult)
        v.tensor_scalar(GAM[:, 2:3], psb[:, 2:3], 1.0 / 127.0, None, OP.mult)
        v.tensor_scalar(GAM[:, 3:4], psb[:, 0:1], 0.5, None, OP.mult)
        v.tensor_scalar(GAM[:, 5:6], psb[:, 1:2], 0.5, None, OP.mult)
        v.tensor_scalar(GAM2[:, 0:1], psb[:, 2:3], 0.5, None, OP.mult)
        spacer(v)
        v.tensor_scalar(GAM[:, 4:5], GAM[:, 3:4], -1.0, None, OP.mult)
        v.tensor_scalar(GAM[:, 6:7], GAM[:, 5:6], -1.0, None, OP.mult)
        return v.tensor_scalar(GAM2[:, 1:2], GAM2[:, 0:1], -1.0, None, OP.mult)
    nv2 = run("vector", fgam2, sV)
    wait("tensor", sV, nv2)
    barrier()
    S2a.close()

    S3 = open_stack()
    WOs = sb(S3, "WOs", [128, NH * D], F16)
    XR = sb(S3, "XR", [128, D], F32)
    RT = sb(S3, "RT", [128, D], F32)
    for t in range(NH):
        dma("gpsimd", dG, WOs[:, t * D:(t + 1) * D], wo[t * 128:(t + 1) * 128, :])
    barrier()

    for t in range(NT):
        nx = dma("sync", dL, XR[:, :], xb[t * 128:(t + 1) * 128, :])
        wait("vector", dL, nx)
        for dgrp in range(4):
            pb = PSL[dgrp % 2]
            def mmo(te, t=t, dgrp=dgrp, pb=pb):
                r = None
                for h in range(NH):
                    r = te.matmul(pb[:, :],
                                  ATT[:, h * OWN + t * 128:h * OWN + (t + 1) * 128],
                                  WOs[:, h * D + dgrp * 512:h * D + (dgrp + 1) * 512],
                                  start=(h == 0), stop=(h == NH - 1))
                return r
            nmm = run("tensor", mmo, sT)
            wait("vector", sT, nmm)
            nev = run("vector", lambda v, dgrp=dgrp, pb=pb: v.tensor_tensor(
                RT[:, dgrp * 512:(dgrp + 1) * 512], pb[:, :],
                XR[:, dgrp * 512:(dgrp + 1) * 512], OP.add), sV)
            wait("tensor", sV, nev)
        nrt = sV.n
        wait("scalar", sV, nrt)
        nd = dma("scalar", dS, r_hbm[t * 128:(t + 1) * 128, :], RT[:, :])
        wait("vector", dS, nd)
        wait("sync", dS, nd)
    barrier()
    dbg_snap("r")
    S3.close()
    S2b.close()

    S4b = open_stack()
    XQT = sb(S4b, "XQT", [128, DT * OWN], F16)
    S4a = open_stack()
    LN2 = sb(S4a, "LN2", [128, D], F32)
    V0B = sb(S4a, "V0B", [128, D], F32)
    WKB = sb(S4a, "WKB", [128, D], F32)
    BIT = sb(S4a, "BIT", [128, D], F32)
    XQN = sb(S4a, "XQN", [128, D], F16)
    XR4 = sb(S4a, "XR4", [128, D], F32)

    for t in range(NT):
        nr = dma("sync", dL, XR4[:, :], r_hbm[t * 128:(t + 1) * 128, :])
        wait("vector", dL, nr)
        wait("scalar", dL, nr)
        nln = ln_tile(XR4[:, :], LN2[:, :])
        wait("vector", sC, nln)
        def fgx(v, t=t):
            v.reduce_max(COL[:, 0:1], LN2[:, :], axis=AX.X,
                         apply_absolute_value=True)
            spacer(v)
            v.tensor_tensor(GX[:, t:t + 1], COL[:, 0:1], CONST[:, 4:5],
                            OP.max)
            spacer(v)
            return v.nop()
        run("vector", fgx)
        run("vector", recip_col(COL[:, 1:2], GX[:, t:t + 1], scale=127.0))
        def fq(v):
            spacer(v)
            v.tensor_scalar(V0B[:, :], LN2[:, :], COL[:, 1:2], 128.5,
                            OP.mult, OP.add)
            v.tensor_copy(WKB[:, :], V0B[:, :])
            for k in range(7, -1, -1):
                v.tensor_single_scalar(BIT[:, :], WKB[:, :], float(2 ** k), OP.is_ge)
                v.scalar_tensor_tensor(WKB[:, :], BIT[:, :], float(-(2 ** k)),
                                       WKB[:, :], OP.mult, OP.add)
            v.tensor_tensor(V0B[:, :], V0B[:, :], WKB[:, :], OP.subtract)
            return v.tensor_scalar(XQN[:, :], V0B[:, :], -128.0, None, OP.add)
        nq = run("vector", fq, sV)
        wait("scalar", sV, nq)
        nd = dma("scalar", dS, xq_hbm[t * 128:(t + 1) * 128, :], XQN[:, :])
        wait("vector", dS, nd)
        wait("sync", dS, nd)
    ntr = None
    for c in range(DT):
        ntr = dma("sync", dL, XQT[:, c * OWN:(c + 1) * OWN],
                  xq_hbm[:, c * 128:(c + 1) * 128], transpose=True)
    wait("tensor", dL, ntr)
    wait("vector", dL, ntr)
    if DEBUG:
        dma("sync", dL, dbg_out[:, 112:128], V0B[:, 0:16])
        dma("sync", dL, dbg_out[:, 128:144], WKB[:, 0:16])
        dma("sync", dL, dbg_out[:, 144:160], LN2[:, 0:16])
        dma("sync", dL, dbg_out[:, 160:162], COL[:, 0:2])
    barrier()
    dbg_snap("xq")
    S4a.close()

    S5 = open_stack()
    GWf = sb(S5, "GWf", [128, 2 * 512], F32)
    VWf = sb(S5, "VWf", [128, 2 * 512], F32)
    TGW = sb(S5, "TGW", [128, DT * 512], F16)
    TVW = sb(S5, "TVW", [128, DT * 512], F16)
    GBR = sb(S5, "GBR", [128, HID], BF16)
    VBR = sb(S5, "VBR", [128, HID], BF16)
    BRS = sb(S5, "BRS", [1, 512], F32)
    GT5 = sb(S5, "GT5", [128, 512], F32)
    VT5 = sb(S5, "VT5", [128, 512], F32)
    GV5 = sb(S5, "GV5", [128, 512], F16)
    GS5 = sb(S5, "GS5", [128, 512], F32)

    for cc2 in range(HID // 512):
        nbr = dma("sync", dL, BRS[:, :], gb[0:1, cc2 * 512:(cc2 + 1) * 512])
        wait("tensor", dL, nbr)
        pb = PSL[cc2 % 2]
        nmm = run("tensor", lambda t, pb=pb: t.matmul(
            pb[:, :], ONE1[:, :], BRS[:, :], start=True, stop=True), sT)
        wait("vector", sT, nmm)
        nev = run("vector", lambda v, cc2=cc2, pb=pb: v.tensor_copy(
            GBR[:, cc2 * 512:(cc2 + 1) * 512], pb[:, :]), sV)
        wait("tensor", sV, nev)
        wait("sync", sV, nev)
    for cc2 in range(HID // 512):
        nbr = dma("sync", dL, BRS[:, :], vb[0:1, cc2 * 512:(cc2 + 1) * 512])
        wait("tensor", dL, nbr)
        pb = PSL[cc2 % 2]
        nmm = run("tensor", lambda t, pb=pb: t.matmul(
            pb[:, :], ONE1[:, :], BRS[:, :], start=True, stop=True), sT)
        wait("vector", sT, nmm)
        nev = run("vector", lambda v, cc2=cc2, pb=pb: v.tensor_copy(
            VBR[:, cc2 * 512:(cc2 + 1) * 512], pb[:, :]), sV)
        wait("tensor", sV, nev)
        wait("sync", sV, nev)
    for cc2 in range(D // 512):
        nbr = dma("sync", dL, BRS[:, :], ob[0:1, cc2 * 512:(cc2 + 1) * 512])
        wait("tensor", dL, nbr)
        pb = PSL[cc2 % 2]
        nmm = run("tensor", lambda t, pb=pb: t.matmul(
            pb[:, :], ONE1[:, :], BRS[:, :], start=True, stop=True), sT)
        wait("vector", sT, nmm)
        nev = run("vector", lambda v, cc2=cc2, pb=pb: v.tensor_copy(
            OUTB[:, cc2 * 512:(cc2 + 1) * 512], pb[:, :]), sV)
        wait("tensor", sV, nev)
        wait("sync", sV, nev)
    def fsg(v):
        for t in range(NT):
            v.tensor_tensor(SGc[:, t:t + 1], GX[:, t:t + 1], GAM[:, 0:1], OP.mult)
            v.tensor_tensor(SVc[:, t:t + 1], GX[:, t:t + 1], GAM[:, 1:2], OP.mult)
        return v.nop()
    run("vector", fsg)

    sT_chunk_end = 0
    for ch in range(HID // 512):
        wait("vector", sT, sT_chunk_end)
        nquant = None
        for c in range(DT):
            p = c % 2
            ng5 = dma("sync", dL, GWf[:, p * 512:(p + 1) * 512],
                      gwT[c * 128:(c + 1) * 128, ch * 512:(ch + 1) * 512])
            nv5 = dma("sync", dL, VWf[:, p * 512:(p + 1) * 512],
                      vwT[c * 128:(c + 1) * 128, ch * 512:(ch + 1) * 512])
            wait("vector", dL, nv5)
            def fquant(v, c=c, p=p):
                gsl = GWf[:, p * 512:(p + 1) * 512]
                v.tensor_scalar(GT5[:, :], gsl, GAM[:, 3:4], None, OP.is_ge)
                v.tensor_scalar(VT5[:, :], gsl, GAM[:, 4:5], None, OP.is_le)
                v.tensor_tensor(TGW[:, c * 512:(c + 1) * 512], GT5[:, :],
                                VT5[:, :], OP.subtract)
                vsl = VWf[:, p * 512:(p + 1) * 512]
                v.tensor_scalar(GT5[:, :], vsl, GAM[:, 5:6], None, OP.is_ge)
                v.tensor_scalar(VT5[:, :], vsl, GAM[:, 6:7], None, OP.is_le)
                return v.tensor_tensor(TVW[:, c * 512:(c + 1) * 512], GT5[:, :],
                                       VT5[:, :], OP.subtract)
            nquant = run("vector", fquant, sV)
            wait("sync", sV, nquant)
        wait("tensor", sV, nquant)
        for t in range(NT):
            pg = PSL[t % 2]
            pv = PSL[2 + t % 2]
            def mmg(te, t=t, pg=pg):
                r = None
                for c in range(DT):
                    r = te.matmul(pg[:, :],
                                  XQT[:, c * OWN + t * 128:c * OWN + (t + 1) * 128],
                                  TGW[:, c * 512:(c + 1) * 512],
                                  start=(c == 0), stop=(c == DT - 1))
                return r
            nmg = run("tensor", mmg, sT)
            def mmv2(te, t=t, pv=pv):
                r = None
                for c in range(DT):
                    r = te.matmul(pv[:, :],
                                  XQT[:, c * OWN + t * 128:c * OWN + (t + 1) * 128],
                                  TVW[:, c * 512:(c + 1) * 512],
                                  start=(c == 0), stop=(c == DT - 1))
                return r
            nmv = run("tensor", mmv2, sT)
            wait("vector", sT, nmg)
            nstg = run("vector", lambda v, t=t, ch=ch, pg=pg: v.scalar_tensor_tensor(
                GT5[:, :], pg[:, :], SGc[:, t:t + 1],
                GBR[:, ch * 512:(ch + 1) * 512], OP.mult, OP.add), sV)
            wait("scalar", sV, nstg)
            nsl = run("scalar", lambda s: s.activation(
                GS5[:, :], GT5[:, :], AF.Silu), sC)
            wait("vector", sT, nmv)
            nstv = run("vector", lambda v, t=t, ch=ch, pv=pv: v.scalar_tensor_tensor(
                VT5[:, :], pv[:, :], SVc[:, t:t + 1],
                VBR[:, ch * 512:(ch + 1) * 512], OP.mult, OP.add), sV)
            wait("tensor", sV, nstv)
            wait("vector", sC, nsl)
            par = ch % 2
            def fgv(v, t=t, par=par):
                v.tensor_tensor(GV5[:, :], GS5[:, :], VT5[:, :], OP.mult)
                v.reduce_max(COL[:, 0:1], GV5[:, :], axis=AX.X,
                             apply_absolute_value=True)
                spacer(v)
                src_m = GMX if par == 0 else GMXb
                dst_m = GMXb if par == 0 else GMX
                return v.tensor_tensor(dst_m[:, t:t + 1], src_m[:, t:t + 1],
                                       COL[:, 0:1], OP.max)
            ngv = run("vector", fgv, sV)
            wait("scalar", sV, ngv)
            nd = dma("scalar", dS,
                     gv_hbm[t * 128:(t + 1) * 128, ch * 512:(ch + 1) * 512],
                     GV5[:, :])
            wait("vector", dS, nd)
        sT_chunk_end = sT.n
    barrier()
    dbg_snap("gv")
    S5.close()
    S4b.close()

    S6 = open_stack()
    TOW = sb(S6, "TOW", [128, NHC * 1024], F16)
    OWf6 = sb(S6, "OWf6", [128, 1024], F32)
    GQT = sb(S6, "GQT", [128, 2 * NHC * 128], F16)
    SRr = sb(S6, "SRr", [1, 512], F32)
    GVT6 = sb(S6, "GVT6", [128, 2 * 512], F16)
    U6 = sb(S6, "U6", [128, 512], F32)
    V06 = sb(S6, "V06", [128, 512], F32)
    W6 = sb(S6, "W6", [128, 512], F32)
    B6 = sb(S6, "B6", [128, 512], F32)
    GQ6 = sb(S6, "GQ6", [128, 512], F16)
    YT = sb(S6, "YT", [128, 1024], F32)
    RT6 = sb(S6, "RT6", [128, 1024], F32)

    def fp6(v):
        for t in range(NT):
            v.tensor_tensor(GMX2[:, t:t + 1], GMX[:, t:t + 1], CONST[:, 4:5],
                            OP.max)
        spacer(v)
        for rep in range(4):
            v.tensor_copy(TRASH[:, rep * 16:(rep + 1) * 16], GMX2[:, :])
        spacer(v)
        v.reciprocal(TRASH[:, 64:128], TRASH[:, 0:64])
        spacer(v)
        v.tensor_scalar(SQc[:, :], TRASH[:, 64:80], 127.0, None, OP.mult)
        for t in range(NT):
            v.tensor_tensor(SOUTc[:, t:t + 1], GMX2[:, t:t + 1], GAM[:, 2:3],
                            OP.mult)
        return v.nop()
    np6 = run("vector", fp6, sV)
    wait("tensor", sV, np6)
    wait("scalar", sV, np6)

    for i in range(NHC):
        for colh in range(2):
            no = dma("sync", dL, OWf6[:, :],
                     owT[i * 128:(i + 1) * 128, colh * 1024:(colh + 1) * 1024])
            wait("vector", dL, no)
            def fqo(v, i=i, colh=colh):
                v.tensor_scalar(GT5[:, :], OWf6[:, 0:512], GAM2[:, 0:1], None,
                                OP.is_ge)
                v.tensor_scalar(VT5[:, :], OWf6[:, 0:512], GAM2[:, 1:2], None,
                                OP.is_le)
                dst1 = TOW[:, i * 1024:i * 1024 + 512] if colh == 0 else GV5[:, :]
                v.tensor_tensor(dst1, GT5[:, :], VT5[:, :], OP.subtract)
                v.tensor_scalar(GT5[:, :], OWf6[:, 512:1024], GAM2[:, 0:1], None,
                                OP.is_ge)
                v.tensor_scalar(VT5[:, :], OWf6[:, 512:1024], GAM2[:, 1:2], None,
                                OP.is_le)
                dst2 = (TOW[:, i * 1024 + 512:(i + 1) * 1024] if colh == 0
                        else GQ6[:, :])
                return v.tensor_tensor(dst2, GT5[:, :], VT5[:, :], OP.subtract)
            nq = run("vector", fqo, sV)
            wait("sync", sV, nq)
            if colh == 1:
                wait("scalar", sV, nq)
                nd = dma("scalar", dS, tow1_hbm[:, i * 1024:i * 1024 + 512],
                         GV5[:, :])
                nd = dma("scalar", dS,
                         tow1_hbm[:, i * 1024 + 512:(i + 1) * 1024], GQ6[:, :])
                wait("vector", dS, nd)

    nfu_end = [0, 0]
    for gq in range(4):
        psr = PSL[7]
        for s4 in range(4):
            t = gq * 4 + s4
            run("tensor", lambda te, t=t, s4=s4, psr=psr: te.matmul(
                psr[0:1, s4 * 128:(s4 + 1) * 128], SQc[:, t:t + 1], IDEN[:, :],
                start=(s4 == 0), stop=(s4 == 3)), sT)
        nsrow = sT.n
        wait("vector", sT, nsrow)
        nsr = run("vector", lambda v, psr=psr: v.tensor_copy(
            SRr[:, :], psr[0:1, :]), sV)
        wait("tensor", sV, nsr)
        psb6 = PSL[6]
        nbc = run("tensor", lambda t, psb6=psb6: t.matmul(
            psb6[:, :], ONE1[:, :], SRr[:, :], start=True, stop=True), sT)
        wait("vector", sT, nbc)
        for i in range(NHC):
            p = i % 2
            wait("sync", sV, nfu_end[p])
            ngt = dma("sync", dL, GVT6[:, p * 512:(p + 1) * 512],
                      gv_hbm[gq * 512:(gq + 1) * 512, i * 128:(i + 1) * 128],
                      transpose=True)
            wait("vector", dL, ngt)
            def fu(v, p=p, psb6=psb6):
                v.tensor_tensor(U6[:, :], GVT6[:, p * 512:(p + 1) * 512],
                                psb6[:, :], OP.mult)
                v.tensor_scalar(V06[:, :], U6[:, :], 128.5, None, OP.add)
                return v.tensor_copy(W6[:, :], V06[:, :])
            nfu = run("vector", fu, sV)
            nfu_end[p] = nfu
            wait("scalar", sV, nfu)
            for k in range(7, -1, -1):
                nsg2 = run("scalar", lambda s: s.activation(
                    B6[:, :], W6[:, :], AF.Sign, bias=CONST[:, 3:4],
                    scale=CONST[:, 2:3]), sC)
                wait("vector", sC, nsg2)
                coeff = -(2.0 ** (k - 1))
                npl = run("vector", lambda v, coeff=coeff: v.scalar_tensor_tensor(
                    W6[:, :], B6[:, :], coeff, W6[:, :], OP.mult, OP.add), sV)
                if k > 0:
                    wait("scalar", sV, npl)
            def ffin(v):
                v.tensor_tensor(V06[:, :], V06[:, :], W6[:, :], OP.subtract)
                return v.tensor_scalar(GQ6[:, :], V06[:, :], -0.5, None, OP.add)
            nfi = run("vector", ffin, sV)
            wait("scalar", sV, nfi)
            nd = dma("scalar", dS,
                     gvq_hbm[:, i * OWN + gq * 512:i * OWN + (gq + 1) * 512],
                     GQ6[:, :])
            wait("vector", dS, nd)
        wait("tensor", dS, dS.n)
    barrier()
    dbg_snap("gvq")

    for half in range(2):
        if half == 1:
            wait("sync", sT, sT.n)
            ntw = dma("sync", dL, TOW[:, :], tow1_hbm[:, :])
            wait("tensor", dL, ntw)
        ngq = [0, 0]
        sT_tile_end = [0] * (NT + 2)

        def load_gq(t):
            p = t % 2
            src = gvq_hbm[:, :].rearrange("p (i w) -> p i w", i=NHC)[
                :, :, t * 128:(t + 1) * 128]
            dstv = GQT[:, p * NHC * 128:(p + 1) * NHC * 128].rearrange(
                "p (i w) -> p i w", i=NHC)
            return dma("sync", dL, dstv, src)

        for t in range(NT):
            p = t % 2
            wait("sync", sT, sT_tile_end[t])
            ngq[p] = load_gq(t)
            nr = dma("sync", dL, RT6[:, :],
                     r_hbm[t * 128:(t + 1) * 128, half * 1024:(half + 1) * 1024])
            wait("tensor", dL, ngq[p])
            for dgrp in range(2):
                pb = PSL[dgrp]
                def mmo6(te, p=p, dgrp=dgrp, pb=pb):
                    r = None
                    for i in range(NHC):
                        r = te.matmul(
                            pb[:, :],
                            GQT[:, p * NHC * 128 + i * 128:p * NHC * 128 + (i + 1) * 128],
                            TOW[:, i * 1024 + dgrp * 512:i * 1024 + (dgrp + 1) * 512],
                            start=(i == 0), stop=(i == NHC - 1))
                    return r
                nmm = run("tensor", mmo6, sT)
                wait("vector", sT, nmm)
                wait("vector", dL, nr)
                def fep(v, t=t, dgrp=dgrp, half=half, pb=pb):
                    c6 = dgrp * 512
                    v.scalar_tensor_tensor(YT[:, c6:c6 + 512], pb[:, :],
                                           SOUTc[:, t:t + 1],
                                           OUTB[:, half * 1024 + c6:
                                                half * 1024 + c6 + 512],
                                           OP.mult, OP.add)
                    return v.tensor_tensor(YT[:, c6:c6 + 512], YT[:, c6:c6 + 512],
                                           RT6[:, c6:c6 + 512], OP.add)
                nep = run("vector", fep, sV)
                wait("tensor", sV, nep)
            sT_tile_end[t + 2] = sT.n
            nyv = sV.n
            wait("scalar", sV, nyv)
            nd = dma("scalar", dS,
                     y_out[t * 128:(t + 1) * 128, half * 1024:(half + 1) * 1024],
                     YT[:, :])
            wait("vector", dS, nd)
            wait("sync", dS, nd)
    barrier()
    if DEBUG:
        dma("sync", dL, dbg_out[:, 0:8], GAM[:, :])
        dma("sync", dL, dbg_out[:, 8:10], GAM2[:, :])
        dma("sync", dL, dbg_out[:, 200:208], WACC[:, :])
        dma("sync", dL, dbg_out[:, 16:32], GX[:, :])
        dma("sync", dL, dbg_out[:, 32:48], SGc[:, :])
        dma("sync", dL, dbg_out[:, 48:64], SVc[:, :])
        dma("sync", dL, dbg_out[:, 64:80], GMX[:, :])
        dma("sync", dL, dbg_out[:, 80:96], SQc[:, :])
        dma("sync", dL, dbg_out[:, 96:112], SOUTc[:, :])
        def fdone(eng):
            eng.wait_ge(dL.h, dL.n)
        emit("sync", fdone)
    barrier()

    with root:
        blk = root.enter_context(nc.Block())
        @blk.sync
        def _(eng):
            for f in sched["sync"]:
                f(eng)
        @blk.tensor
        def _(eng):
            for f in sched["tensor"]:
                f(eng)
        @blk.vector
        def _(eng):
            for f in sched["vector"]:
                f(eng)
        @blk.scalar
        def _(eng):
            for f in sched["scalar"]:
                f(eng)
        @blk.gpsimd
        def _(eng):
            for f in sched["gpsimd"]:
                f(eng)
        for s in reversed(phase_stacks):
            s.close()
    return nc


def _host_prep(inputs):
    x = np.ascontiguousarray(inputs["L0_attn_x"])
    f32 = np.float32
    shared = {
        "wdq": np.ascontiguousarray(inputs["L0_attn_W_D_Q"], dtype=f32),
        "wuq": np.ascontiguousarray(inputs["L0_attn_W_U_Q"], dtype=f32),
        "wdkv": np.ascontiguousarray(inputs["L0_attn_W_D_KV"], dtype=f32),
        "wuk": np.ascontiguousarray(inputs["L0_attn_W_U_K"], dtype=f32),
        "wuv": np.ascontiguousarray(inputs["L0_attn_W_U_V"], dtype=f32),
        "wo": np.ascontiguousarray(inputs["L0_attn_W_O"], dtype=f32),
        "gwT": np.ascontiguousarray(np.asarray(inputs["L0_ffn_gate_w"]).T),
        "vwT": np.ascontiguousarray(np.asarray(inputs["L0_ffn_val_w"]).T),
        "owT": np.ascontiguousarray(np.asarray(inputs["L0_ffn_out_w"]).T),
        "gb": np.ascontiguousarray(np.asarray(inputs["L0_ffn_gate_b"]).reshape(1, HID), dtype=f32),
        "vb": np.ascontiguousarray(np.asarray(inputs["L0_ffn_val_b"]).reshape(1, HID), dtype=f32),
        "ob": np.ascontiguousarray(np.asarray(inputs["L0_ffn_out_b"]).reshape(1, D), dtype=f32),
    }
    permA = (np.arange(NH)[:, None] * DHR + np.arange(32)[None, :]).reshape(-1)
    permB = (np.arange(NH)[:, None] * DHR + 32 + np.arange(32)[None, :]).reshape(-1)
    perm = np.concatenate([permA, permB])
    shared["wqr"] = np.ascontiguousarray(np.asarray(inputs["L0_attn_W_Q_R"])[:, perm], dtype=f32)
    shared["wkr"] = np.ascontiguousarray(np.asarray(inputs["L0_attn_W_K_R"])[:, perm], dtype=f32)

    inv_freq = 10000.0 ** (-(np.arange(0, DHR, 2, dtype=np.float64)) / DHR)
    pos = np.arange(L, dtype=np.float64)
    ang = pos[None, :] * inv_freq[:, None]
    cosk = np.tile(np.cos(ang), (4, 1)).astype(f32)
    sink = np.tile(np.sin(ang), (4, 1)).astype(f32)

    in_maps = []
    for core in range(8):
        b, qh = core // 2, core % 2
        m = dict(shared)
        xbv = np.asarray(x[b], dtype=f32)
        if qh == 1:
            xbv = np.concatenate([xbv[OWN:], xbv[:OWN]], axis=0)
            ck = np.concatenate([cosk[:, OWN:], cosk[:, :OWN]], axis=1)
            sk = np.concatenate([sink[:, OWN:], sink[:, :OWN]], axis=1)
        else:
            ck, sk = cosk, sink
        m["xb"] = np.ascontiguousarray(xbv)
        m["cosk"] = np.ascontiguousarray(ck)
        m["sink"] = np.ascontiguousarray(sk)
        m["cosq"] = np.ascontiguousarray(ck[:, 0:OWN])
        m["sinq"] = np.ascontiguousarray(sk[:, 0:OWN])
        in_maps.append(m)
    return in_maps


_NC_CACHE = {}


def kernel(**inputs):
    in_maps = _host_prep(inputs)
    if "nc" not in _NC_CACHE:
        _NC_CACHE["nc"] = build_program()
    nc = _NC_CACHE["nc"]
    res = run_bass_kernel_spmd(nc, in_maps, core_ids=list(range(8)))
    y = np.empty((B, L, D), dtype=np.float32)
    for core in range(8):
        b, qh = core // 2, core % 2
        y[b, qh * OWN:(qh + 1) * OWN, :] = np.asarray(res.results[core]["y"])
    return y


# version 04
# speedup vs baseline: 1.0016x; 1.0009x over previous
import contextlib
import numpy as np

import concourse.bass as bass
import concourse.mybir as mybir
from concourse.bass_utils import run_bass_kernel_spmd

F32 = mybir.dt.float32
BF16 = mybir.dt.bfloat16
F16 = mybir.dt.float16
AF = mybir.ActivationFunctionType
OP = mybir.AluOpType
AX = mybir.AxisListType

B, L, D = 4, 4096, 2048
NH, DH, DHR, DC, DCQ = 16, 128, 64, 512, 1536
HID = 8192
OWN = 2048
NG = L // 512
NGO = OWN // 512
NT = OWN // 128
DT = D // 128
KQ = DCQ // 128
KC = DC // 128
NHC = HID // 128
SCALE = DH ** -0.5
EXP_SHIFT = 0.0
EPS = 1e-5
N_ENG = 5
DEBUG = False
P1A_PARTS = ("qc", "c", "kr")


class Sem:
    def __init__(self, handle):
        self.h = handle
        self.n = 0

    def inc(self, k=1):
        self.n += k
        return self.n


def build_program(stop_after=None):
    nc = bass.Bass("TRN2", target_bir_lowering=False)
    sched = {e: [] for e in ("sync", "tensor", "vector", "scalar", "gpsimd")}

    def emit(e, fn):
        sched[e].append(fn)

    def inp(name, shape, dtype=F32):
        return nc.dram_tensor(name, shape, dtype, kind="ExternalInput")

    xb = inp("xb", [L, D])
    wdq = inp("wdq", [D, DCQ])
    wuq = inp("wuq", [DCQ, NH * DH])
    wqr = inp("wqr", [DCQ, NH * DHR])
    wdkv = inp("wdkv", [D, DC])
    wuk = inp("wuk", [DC, NH * DH])
    wkr = inp("wkr", [D, NH * DHR])
    wuv = inp("wuv", [DC, NH * DH])
    wo = inp("wo", [NH * DH, D])
    gwT = inp("gwT", [D, HID])
    vwT = inp("vwT", [D, HID])
    owT = inp("owT", [HID, D])
    gb = inp("gb", [1, HID])
    vb = inp("vb", [1, HID])
    ob = inp("ob", [1, D])
    cosk = inp("cosk", [128, L])
    sink = inp("sink", [128, L])
    cosq = inp("cosq", [128, OWN])
    sinq = inp("sinq", [128, OWN])

    y_out = nc.dram_tensor("y", [OWN, D], F32, kind="ExternalOutput")
    dbg_out = None
    dtw = {}
    if DEBUG:
        dbg_out = nc.dram_tensor("dbg", [128, 256], F32, kind="ExternalOutput")

    ln_hbm = nc.dram_tensor("ln_s", [L, D], F16, kind="Internal")
    qcT_hbm = nc.dram_tensor("qcT_s", [128, KQ * OWN], F16, kind="Internal")
    cT_hbm = nc.dram_tensor("cT_s", [128, KC * L], F16, kind="Internal")
    k1_hbm = nc.dram_tensor("k1_s", [128, NH * L], F16, kind="Internal")
    k2_hbm = nc.dram_tensor("k2_s", [64, NH * L], F16, kind="Internal")
    v_hbm = nc.dram_tensor("v_s", [128, (L // 128) * D], F16, kind="Internal")
    qT_hbm = nc.dram_tensor("qT_s", [192, NH * OWN], F16, kind="Internal")
    r_hbm = nc.dram_tensor("r_s", [OWN, D], F32, kind="Internal")
    xq_hbm = nc.dram_tensor("xq_s", [OWN, D], F16, kind="Internal")
    gv_hbm = nc.dram_tensor("gv_s", [OWN, HID], F16, kind="Internal")
    gvq_hbm = nc.dram_tensor("gvq_s", [128, NHC * OWN], F16, kind="Internal")
    tow1_hbm = nc.dram_tensor("tow1_s", [128, NHC * 1024], F16, kind="Internal")

    if DEBUG:
        for nm, t in (("ln", ln_hbm), ("qcT", qcT_hbm), ("cT", cT_hbm),
                      ("k1", k1_hbm), ("k2", k2_hbm), ("v", v_hbm),
                      ("qT", qT_hbm), ("r", r_hbm), ("xq", xq_hbm),
                      ("gv", gv_hbm), ("gvq", gvq_hbm)):
            shp = [t.shape[0], t.shape[1]] if hasattr(t, "shape") else None
            dtw[nm] = (nc.dram_tensor("d_" + nm, list(t.shape),
                                      t.dtype, kind="ExternalOutput"), t)

    ones_row = nc.inline_tensor(np.ones((1, 128), dtype=np.float32), "ones_row_c")
    iden128 = nc.inline_tensor(np.eye(128, dtype=np.float32), "iden_c")

    root = contextlib.ExitStack()

    def mkstack():
        return root.enter_context(contextlib.ExitStack())

    phase_stacks = []

    def open_stack():
        s = contextlib.ExitStack()
        phase_stacks.append(s)
        return s

    def sb(stack, name, shape, dtype):
        return stack.enter_context(nc.sbuf_tensor(name, shape, dtype))

    S0 = open_stack()
    CONST = sb(S0, "CONST", [128, 8], F32)
    ONE1 = sb(S0, "ONE1", [1, 128], F32)
    IDEN = sb(S0, "IDEN", [128, 128], F32)
    RW8 = sb(S0, "RW8", [128, 8], F32)
    RW8b = sb(S0, "RW8b", [128, 8], F32)
    COL = sb(S0, "COL", [128, 12], F32)
    SPC = sb(S0, "SPC", [128, 4], F32)
    ONES8 = sb(S0, "ONES8", [128, 8], F32)
    GMXb = sb(S0, "GMXb", [128, NT], F32)
    GMX2 = sb(S0, "GMX2", [128, NT], F32)
    GX = sb(S0, "GX", [128, NT], F32)
    GMX = sb(S0, "GMX", [128, NT], F32)
    SGc = sb(S0, "SGc", [128, NT], F32)
    SVc = sb(S0, "SVc", [128, NT], F32)
    SQc = sb(S0, "SQc", [128, NT], F32)
    SOUTc = sb(S0, "SOUTc", [128, NT], F32)
    WACC = sb(S0, "WACC", [128, 8], F32)
    GAM = sb(S0, "GAM", [128, 8], F32)
    GAM2 = sb(S0, "GAM2", [128, 2], F32)
    TRASH = sb(S0, "TRASH", [128, 2048], F32)
    OUTB = sb(S0, "OUTB", [128, D], F32)

    PSL = [S0.enter_context(nc.psum_tensor(f"ps{i}", [128, 512], F32))
           for i in range(8)]

    def mksem(name):
        return Sem(S0.enter_context(nc.semaphore(name)))

    sT = mksem("sT")
    sV = mksem("sV")
    sC = mksem("sC")
    dL = mksem("dL")
    dS = mksem("dS")
    dG = mksem("dG")
    BAR = mksem("BAR")

    ph_no = [0]

    def barrier():
        ph_no[0] += 1
        tgt = N_ENG * ph_no[0]
        flush = {"sync": dL, "scalar": dS, "gpsimd": dG}
        for e in ("sync", "tensor", "vector", "scalar", "gpsimd"):
            fs = flush.get(e)
            fn_n = fs.n if fs else 0
            def f(eng, fs=fs, fn_n=fn_n, tgt=tgt):
                if fs is not None and fn_n > 0:
                    eng.wait_ge(fs.h, fn_n)
                eng.nop().then_inc(BAR.h, 1)
                eng.wait_ge(BAR.h, tgt)
            emit(e, f)
        BAR.inc(N_ENG)

    def dma(qe, qs, out_ap, in_ap, transpose=False):
        n = qs.inc(16)
        def f(eng, out_ap=out_ap, in_ap=in_ap, transpose=transpose):
            eng.dma_start(out_ap, in_ap, transpose=transpose).then_inc(qs.h, 16)
        emit(qe, f)
        return n

    def wait(e, sem, n):
        if n and n > 0:
            def f(eng, n=n, sem=sem):
                eng.wait_ge(sem.h, n)
            emit(e, f)

    def run(e, fn, sem=None):
        if sem is None:
            emit(e, fn)
            return None
        n = sem.inc()
        def f(eng, fn=fn, sem=sem):
            fn(eng).then_inc(sem.h, 1)
        emit(e, f)
        return n

    def dbg_snap(*names):
        if not DEBUG:
            return
        for nm in names:
            dst, srct = dtw[nm]
            dma("sync", dL, dst[:, :], srct[:, :])

    def spacer(v):
        v.tensor_copy(SPC[:, 0:4], CONST[:, 0:4])

    def recip_col(dst_col, src_col, scale=None):
        def f(v, dst=dst_col, src=src_col, scale=scale):
            v.tensor_scalar(RW8b[:, :], ONES8[:, :], src, None, OP.mult)
            spacer(v)
            v.reciprocal(RW8[:, :], RW8b[:, :])
            spacer(v)
            if scale is None:
                return v.tensor_copy(dst, RW8[:, 0:1])
            return v.tensor_scalar(dst, RW8[:, 0:1], float(scale), None, OP.mult)
        return f

    def ln_tile(src_ap, dst_ap, dst2_ap=None):
        wait("vector", sC, sC.n)
        wait("scalar", sV, sV.n)
        def fv(v):
            v.reduce_sum(COL[:, 0:1], src_ap, axis=AX.X)
            spacer(v)
            return v.tensor_scalar(COL[:, 2:3], COL[:, 0:1], 1.0 / D, None, OP.mult)
        run("vector", fv)
        nsq = run("scalar", lambda s: s.activation(
            TRASH[:, :], src_ap, AF.Square, accum_out=COL[:, 1:2]), sC)
        wait("vector", sC, nsq)
        def fv2(v):
            v.tensor_scalar(COL[:, 4:5], COL[:, 1:2], 1.0 / D, None, OP.mult)
            v.tensor_tensor(COL[:, 3:4], COL[:, 2:3], COL[:, 2:3], OP.mult)
            spacer(v)
            v.tensor_tensor(COL[:, 5:6], COL[:, 4:5], COL[:, 3:4], OP.subtract)
            spacer(v)
            return v.tensor_scalar(COL[:, 6:7], COL[:, 5:6], EPS, None, OP.add)
        run("vector", fv2)
        nv2 = run("vector", lambda v: v.nop(), sV)
        wait("scalar", sV, nv2)
        nst = run("scalar", lambda s: s.activation(
            COL[:, 8:9], COL[:, 6:7], AF.Sqrt), sC)
        wait("vector", sC, nst)
        run("vector", recip_col(COL[:, 9:10], COL[:, 8:9]))
        def fng(v):
            spacer(v)
            v.tensor_tensor(COL[:, 10:11], COL[:, 2:3], COL[:, 9:10], OP.mult)
            spacer(v)
            return v.tensor_scalar(COL[:, 11:12], COL[:, 10:11], -1.0, None, OP.mult)
        nng = run("vector", fng, sV)
        wait("scalar", sV, nng)
        nln = run("scalar", lambda s: s.activation(
            dst_ap, src_ap, AF.Identity, bias=COL[:, 11:12], scale=COL[:, 9:10]), sC)
        if dst2_ap is not None:
            nln = run("scalar", lambda s: s.activation(
                dst2_ap, src_ap, AF.Identity,
                bias=COL[:, 11:12], scale=COL[:, 9:10]), sC)
        return nln

    def g0(g):
        g.memset(CONST[:, 0:1], EXP_SHIFT)
        g.memset(CONST[:, 1:2], 0.0)
        g.memset(CONST[:, 2:3], 1.0)
        g.memset(CONST[:, 3:4], -128.0)
        g.memset(CONST[:, 4:5], 1e-5)
        g.memset(WACC[:, :], 0.0)
        g.memset(GMX[:, :], 0.0)
        g.memset(GMXb[:, :], 0.0)
        g.memset(ONES8[:, :], 1.0)
        return g.nop()
    run("gpsimd", g0)
    dma("sync", dL, ONE1[:, :], ones_row[:, :])
    dma("sync", dL, IDEN[:, :], iden128[:, :])

    S1 = open_stack()
    XT = sb(S1, "XT", [128, 2 * D], F32)
    LNB = sb(S1, "LNB", [128, D], F16)
    LNT = sb(S1, "LNT", [128, DT * 512], F16)
    RPF = sb(S1, "RPF", [128, 8 * 512], F32)
    RPO = sb(S1, "RPO", [128, 8 * 512], F16)
    RPS = sb(S1, "RPS", [128, 1024], F32)
    EV1 = sb(S1, "EV1", [128, 512], F16)
    EV2 = sb(S1, "EV2", [128, 512], F16)
    S1a = open_stack()
    WDQs = sb(S1a, "WDQs", [128, DT * DCQ], F16)
    WDKVs = sb(S1a, "WDKVs", [128, DT * DC], F16)
    WKRs = sb(S1a, "WKRs", [128, DT * 1024], F16)
    TBLG = sb(S1a, "TBLG", [128, 2 * 512], F32)

    for t in range(DT):
        dma("gpsimd", dG, WDQs[:, t * DCQ:(t + 1) * DCQ],
            wdq[t * 128:(t + 1) * 128, :])
        dma("gpsimd", dG, WDKVs[:, t * DC:(t + 1) * DC],
            wdkv[t * 128:(t + 1) * 128, :])
        dma("gpsimd", dG, WKRs[:, t * 1024:(t + 1) * 1024],
            wkr[t * 128:(t + 1) * 128, :])
    barrier()

    sT_group_end = 0
    prev_rope = 0
    for g in range(NG):
        c0, c1 = g * 512, (g + 1) * 512
        wait("sync", sT, sT_group_end)
        wait("sync", sV, prev_rope)
        ntb = dma("sync", dL, TBLG[:, 0:512], cosk[:, c0:c1])
        ntb = dma("sync", dL, TBLG[:, 512:1024], sink[:, c0:c1])
        nspill_last = None
        for i in range(4):
            r0 = g * 512 + i * 128
            px = i % 2
            nx = dma("sync", dL, XT[:, px * D:(px + 1) * D], xb[r0:r0 + 128, :])
            wait("vector", dL, nx)
            wait("scalar", dL, nx)
            nln = ln_tile(XT[:, px * D:(px + 1) * D], LNB[:, :])
            wait("scalar", sC, nln)
            if DEBUG and g == 0 and i == 0:
                dma("scalar", dS, dbg_out[:, 176:188], COL[:, 0:12])
            nspill_last = dma("scalar", dS, ln_hbm[r0:r0 + 128, :], LNB[:, :])
            wait("scalar", dS, nspill_last)
            wait("sync", sC, nln)
        wait("sync", dS, nspill_last)
        ntr = None
        for c in range(DT):
            ntr = dma("sync", dL, LNT[:, c * 512:(c + 1) * 512],
                      ln_hbm[c0:c1, c * 128:(c + 1) * 128], transpose=True)
        wait("tensor", dL, ntr)
        wait("vector", dL, ntb)

        if g < NGO and "qc" in P1A_PARTS:
            for i in range(KQ):
                pb = PSL[i % 2]
                def mmq(t, i=i, pb=pb):
                    r = None
                    for c in range(DT):
                        r = t.matmul(pb[:, :],
                                     WDQs[:, c * DCQ + i * 128:c * DCQ + (i + 1) * 128],
                                     LNT[:, c * 512:(c + 1) * 512],
                                     start=(c == 0), stop=(c == DT - 1))
                    return r
                nmm = run("tensor", mmq, sT)
                wait("vector", sT, nmm)
                nev = run("vector", lambda v, pb=pb: v.tensor_copy(EV1[:, :], pb[:, :]), sV)
                wait("tensor", sV, nev)
                wait("scalar", sV, nev)
                nd = dma("scalar", dS,
                         qcT_hbm[:, i * OWN + c0:i * OWN + c1], EV1[:, :])
                wait("vector", dS, nd)
        for i in range(KC if "c" in P1A_PARTS else 0):
            pb = PSL[2 + i % 2]
            def mmc(t, i=i, pb=pb):
                r = None
                for c in range(DT):
                    r = t.matmul(pb[:, :],
                                 WDKVs[:, c * DC + i * 128:c * DC + (i + 1) * 128],
                                 LNT[:, c * 512:(c + 1) * 512],
                                 start=(c == 0), stop=(c == DT - 1))
                return r
            nmm = run("tensor", mmc, sT)
            wait("vector", sT, nmm)
            nev = run("vector", lambda v, pb=pb: v.tensor_copy(EV2[:, :], pb[:, :]), sV)
            wait("tensor", sV, nev)
            wait("scalar", sV, nev)
            nd = dma("scalar", dS, cT_hbm[:, i * L + c0:i * L + c1], EV2[:, :])
            wait("vector", dS, nd)
        for i in range(8 if "kr" in P1A_PARTS else 0):
            pb = PSL[4 + i % 2]
            def mmk(t, i=i, pb=pb):
                r = None
                for c in range(DT):
                    r = t.matmul(pb[:, :],
                                 WKRs[:, c * 1024 + i * 128:c * 1024 + (i + 1) * 128],
                                 LNT[:, c * 512:(c + 1) * 512],
                                 start=(c == 0), stop=(c == DT - 1))
                return r
            nmm = run("tensor", mmk, sT)
            wait("vector", sT, nmm)
            nev = run("vector", lambda v, i=i, pb=pb: v.tensor_copy(
                RPF[:, i * 512:(i + 1) * 512], pb[:, :]), sV)
            wait("tensor", sV, nev)
        sT_group_end = sT.n
        for i in range(4 if "kr" in P1A_PARTS else 0):
            a = RPF[:, i * 512:(i + 1) * 512]
            b2 = RPF[:, (i + 4) * 512:(i + 5) * 512]
            oa = RPO[:, i * 512:(i + 1) * 512]
            ob2 = RPO[:, (i + 4) * 512:(i + 5) * 512]
            def rope(v, a=a, b2=b2, oa=oa, ob2=ob2):
                v.tensor_tensor(RPS[:, 0:512], a, TBLG[:, 0:512], OP.mult)
                v.tensor_tensor(RPS[:, 512:1024], b2, TBLG[:, 512:1024], OP.mult)
                v.tensor_tensor(oa, RPS[:, 0:512], RPS[:, 512:1024], OP.subtract)
                v.tensor_tensor(RPS[:, 0:512], b2, TBLG[:, 0:512], OP.mult)
                v.tensor_tensor(RPS[:, 512:1024], a, TBLG[:, 512:1024], OP.mult)
                return v.tensor_tensor(ob2, RPS[:, 0:512], RPS[:, 512:1024], OP.add)
            run("vector", rope)
        nro = run("vector", lambda v: v.nop(), sV)
        prev_rope = nro
        wait("scalar", sV, nro)
        ndl = None
        for i in range(4 if "kr" in P1A_PARTS else 0):
            for s in range(4):
                h = 4 * i + s
                ndl = dma("scalar", dS, k2_hbm[0:32, h * L + c0:h * L + c1],
                          RPO[s * 32:(s + 1) * 32, i * 512:(i + 1) * 512])
                ndl = dma("scalar", dS, k2_hbm[32:64, h * L + c0:h * L + c1],
                          RPO[s * 32:(s + 1) * 32, (i + 4) * 512:(i + 5) * 512])
        if ndl:
            wait("vector", dS, ndl)
    barrier()
    dbg_snap("ln", "qcT", "cT", "k2")
    S1a.close()
    if stop_after == "p1a":
        with root:
            blk = root.enter_context(nc.Block())
            for e in ("sync", "tensor", "vector", "scalar", "gpsimd"):
                def _mk(e):
                    def _f(eng):
                        for f in sched[e]:
                            f(eng)
                    return _f
                getattr(blk, e)(_mk(e))
            for s in reversed(phase_stacks):
                s.close()
        return nc

    S1b = open_stack()
    WUQs = sb(S1b, "WUQs", [128, KQ * D], F16)
    WQRs = sb(S1b, "WQRs", [128, KQ * 1024], F16)
    QCg = sb(S1b, "QCg", [128, KQ * 512], F16)
    TBQG = sb(S1b, "TBQG", [128, 2 * 512], F32)
    for t in range(KQ):
        dma("gpsimd", dG, WUQs[:, t * D:(t + 1) * D], wuq[t * 128:(t + 1) * 128, :])
        dma("gpsimd", dG, WQRs[:, t * 1024:(t + 1) * 1024],
            wqr[t * 128:(t + 1) * 128, :])
    barrier()

    sT_group_end = 0
    prev_rope = 0
    for g in range(NGO):
        c0, c1 = g * 512, (g + 1) * 512
        wait("sync", sT, sT_group_end)
        wait("sync", sV, prev_rope)
        ntb = dma("sync", dL, TBQG[:, 0:512], cosq[:, c0:c1])
        ntb = dma("sync", dL, TBQG[:, 512:1024], sinq[:, c0:c1])
        in_ap = qcT_hbm[:, :].rearrange("p (i w) -> p i w", i=KQ)[:, :, c0:c1]
        out_ap = QCg[:, :].rearrange("p (i w) -> p i w", i=KQ)
        nq = dma("sync", dL, out_ap, in_ap)
        wait("tensor", dL, nq)
        wait("vector", dL, ntb)
        for h in range(NH):
            pb = PSL[h % 2]
            def mmqb(t, h=h, pb=pb):
                r = None
                for c in range(KQ):
                    r = t.matmul(pb[:, :],
                                 WUQs[:, c * D + h * 128:c * D + (h + 1) * 128],
                                 QCg[:, c * 512:(c + 1) * 512],
                                 start=(c == 0), stop=(c == KQ - 1))
                return r
            nmm = run("tensor", mmqb, sT)
            wait("vector", sT, nmm)
            nev = run("vector", lambda v, pb=pb: v.tensor_copy(EV1[:, :], pb[:, :]), sV)
            wait("tensor", sV, nev)
            wait("scalar", sV, nev)
            nd = dma("scalar", dS, qT_hbm[0:128, h * OWN + c0:h * OWN + c1],
                     EV1[:, :])
            wait("vector", dS, nd)
        for i in range(8):
            pb = PSL[2 + i % 2]
            def mmqr(t, i=i, pb=pb):
                r = None
                for c in range(KQ):
                    r = t.matmul(pb[:, :],
                                 WQRs[:, c * 1024 + i * 128:c * 1024 + (i + 1) * 128],
                                 QCg[:, c * 512:(c + 1) * 512],
                                 start=(c == 0), stop=(c == KQ - 1))
                return r
            nmm = run("tensor", mmqr, sT)
            wait("vector", sT, nmm)
            nev = run("vector", lambda v, i=i, pb=pb: v.tensor_copy(
                RPF[:, i * 512:(i + 1) * 512], pb[:, :]), sV)
            wait("tensor", sV, nev)
        sT_group_end = sT.n
        for i in range(4 if "kr" in P1A_PARTS else 0):
            a = RPF[:, i * 512:(i + 1) * 512]
            b2 = RPF[:, (i + 4) * 512:(i + 5) * 512]
            oa = RPO[:, i * 512:(i + 1) * 512]
            ob2 = RPO[:, (i + 4) * 512:(i + 5) * 512]
            def ropeq(v, a=a, b2=b2, oa=oa, ob2=ob2):
                v.tensor_tensor(RPS[:, 0:512], a, TBQG[:, 0:512], OP.mult)
                v.tensor_tensor(RPS[:, 512:1024], b2, TBQG[:, 512:1024], OP.mult)
                v.tensor_tensor(oa, RPS[:, 0:512], RPS[:, 512:1024], OP.subtract)
                v.tensor_tensor(RPS[:, 0:512], b2, TBQG[:, 0:512], OP.mult)
                v.tensor_tensor(RPS[:, 512:1024], a, TBQG[:, 512:1024], OP.mult)
                return v.tensor_tensor(ob2, RPS[:, 0:512], RPS[:, 512:1024], OP.add)
            run("vector", ropeq)
        nro = run("vector", lambda v: v.nop(), sV)
        prev_rope = nro
        wait("scalar", sV, nro)
        ndl = None
        for i in range(4):
            for s in range(4):
                h = 4 * i + s
                ndl = dma("scalar", dS,
                          qT_hbm[128:160, h * OWN + c0:h * OWN + c1],
                          RPO[s * 32:(s + 1) * 32, i * 512:(i + 1) * 512])
                ndl = dma("scalar", dS,
                          qT_hbm[160:192, h * OWN + c0:h * OWN + c1],
                          RPO[s * 32:(s + 1) * 32, (i + 4) * 512:(i + 5) * 512])
        wait("vector", dS, ndl)
    barrier()
    dbg_snap("qT")
    S1b.close()

    S1c = open_stack()
    WUKs = sb(S1c, "WUKs", [128, KC * D], F16)
    WUVs = sb(S1c, "WUVs", [128, KC * D], F16)
    CTg = sb(S1c, "CTg", [128, KC * 512], F16)
    for t in range(KC):
        dma("gpsimd", dG, WUKs[:, t * D:(t + 1) * D], wuk[t * 128:(t + 1) * 128, :])
        dma("gpsimd", dG, WUVs[:, t * D:(t + 1) * D], wuv[t * 128:(t + 1) * 128, :])
    barrier()

    sT_group_end = 0
    for g in range(NG):
        c0, c1 = g * 512, (g + 1) * 512
        wait("sync", sT, sT_group_end)
        in_ap = cT_hbm[:, :].rearrange("p (i w) -> p i w", i=KC)[:, :, c0:c1]
        out_ap = CTg[:, :].rearrange("p (i w) -> p i w", i=KC)
        ncg = dma("sync", dL, out_ap, in_ap)
        wait("tensor", dL, ncg)
        for h in range(NH):
            pb = PSL[h % 2]
            def mmkb(t, h=h, pb=pb):
                r = None
                for c in range(KC):
                    r = t.matmul(pb[:, :],
                                 WUKs[:, c * D + h * 128:c * D + (h + 1) * 128],
                                 CTg[:, c * 512:(c + 1) * 512],
                                 start=(c == 0), stop=(c == KC - 1))
                return r
            nmm = run("tensor", mmkb, sT)
            wait("vector", sT, nmm)
            nev = run("vector", lambda v, pb=pb: v.tensor_copy(EV1[:, :], pb[:, :]), sV)
            wait("tensor", sV, nev)
            wait("scalar", sV, nev)
            nd = dma("scalar", dS, k1_hbm[:, h * L + c0:h * L + c1], EV1[:, :])
            wait("vector", dS, nd)
        for s in range(4):
            tt = 4 * g + s
            for dgrp in range(4):
                pb = PSL[2 + dgrp % 2]
                def mmv(t, s=s, dgrp=dgrp, pb=pb):
                    r = None
                    for c in range(KC):
                        r = t.matmul(pb[:, :],
                                     CTg[:, c * 512 + s * 128:c * 512 + (s + 1) * 128],
                                     WUVs[:, c * D + dgrp * 512:c * D + (dgrp + 1) * 512],
                                     start=(c == 0), stop=(c == KC - 1))
                    return r
                nmm = run("tensor", mmv, sT)
                wait("vector", sT, nmm)
                nev = run("vector", lambda v, pb=pb: v.tensor_copy(EV2[:, :], pb[:, :]), sV)
                wait("tensor", sV, nev)
                wait("scalar", sV, nev)
                nd = dma("scalar", dS,
                         v_hbm[:, tt * D + dgrp * 512:tt * D + (dgrp + 1) * 512],
                         EV2[:, :])
                wait("vector", dS, nd)
        sT_group_end = sT.n
    barrier()
    dbg_snap("k1", "v")
    S1c.close()
    S1.close()

    S2b = open_stack()
    ATT = sb(S2b, "ATT", [128, NH * OWN], F16)
    S2a = open_stack()
    K1h = sb(S2a, "K1h", [128, L], F16)
    K2h = sb(S2a, "K2h", [64, L], F16)
    Vh = sb(S2a, "Vh", [128, L], F16)
    QBh = sb(S2a, "QBh", [128, OWN], F16)
    QRh = sb(S2a, "QRh", [64, OWN], F16)
    PT = sb(S2a, "PT", [128, 3 * 512], F16)
    DN = sb(S2a, "DN", [128, 4 * 512], F32)
    DNr = sb(S2a, "DNr", [1, 512], F32)
    RECB = sb(S2a, "RECB", [128, 512], F32)
    WMP = sb(S2a, "WMP", [128, D], F32)

    mp_list = []
    for mi, (w, nrow, ncol) in enumerate(
            ((gwT, D, HID), (vwT, D, HID), (owT, HID, D))):
        for r in range(nrow // 128):
            for cc2 in range(ncol // D):
                mp_list.append((mi, w, r, cc2))
    mp_idx = [0]
    mp_par = [0, 0, 0]

    def emit_meanpass(k):
        for _ in range(k):
            if mp_idx[0] >= len(mp_list):
                return
            mi, w, r, cc2 = mp_list[mp_idx[0]]
            mp_idx[0] += 1
            par = mp_par[mi]
            mp_par[mi] ^= 1
            nmp = dma("gpsimd", dG, WMP[:, :],
                      w[r * 128:(r + 1) * 128, cc2 * D:(cc2 + 1) * D])
            wait("vector", dG, nmp)
            def fred(v, mi=mi, par=par):
                v.reduce_sum(COL[:, 7:8], WMP[:, :], axis=AX.X,
                             apply_absolute_value=True)
                spacer(v)
                return v.tensor_tensor(WACC[:, 4 * (1 - par) + mi:4 * (1 - par) + mi + 1],
                                       WACC[:, 4 * par + mi:4 * par + mi + 1],
                                       COL[:, 7:8], OP.add)
            nred = run("vector", fred, sV)
            wait("gpsimd", sV, nred)

    sT_head_end = [0] * (NH + 1)
    nld = [None] * NH

    def load_head(h):
        n1 = dma("sync", dL, K1h[:, :], k1_hbm[:, h * L:(h + 1) * L])
        n1 = dma("sync", dL, K2h[:, :], k2_hbm[:, h * L:(h + 1) * L])
        vin = v_hbm[:, :].rearrange("p (k w) -> p k w", k=L // 128)[
            :, :, h * 128:(h + 1) * 128]
        vout = Vh[:, :].rearrange("p (k w) -> p k w", k=L // 128)
        n1 = dma("sync", dL, vout, vin)
        n1 = dma("sync", dL, QBh[:, :], qT_hbm[0:128, h * OWN:(h + 1) * OWN])
        n1 = dma("sync", dL, QRh[:, :], qT_hbm[128:192, h * OWN:(h + 1) * OWN])
        return n1

    for h in range(NH):
        wait("sync", sT, sT_head_end[h])
        nld[h] = load_head(h)
        wait("tensor", dL, nld[h])
        emit_meanpass(12)
        for qt in range(4):
            q0 = qt * 512
            pso = PSL[2 + qt]
            for kt in range(L // 128):
                psx = PSL[kt % 2]
                j = kt % 3
                def mms(t, kt=kt, q0=q0, psx=psx):
                    t.matmul(psx[:, :], K1h[:, kt * 128:(kt + 1) * 128],
                             QBh[:, q0:q0 + 512], start=True, stop=False)
                    return t.matmul(psx[:, :],
                                    K2h[0:64, kt * 128:(kt + 1) * 128],
                                    QRh[0:64, q0:q0 + 512],
                                    start=False, stop=True)
                nmm = run("tensor", mms, sT)
                wait("scalar", sT, nmm)
                nex = run("scalar", lambda s, j=j, psx=psx: s.activation(
                    PT[:, j * 512:(j + 1) * 512], psx[:, :], AF.Exp,
                    bias=CONST[:, 0:1], scale=SCALE), sC)
                wait("tensor", sC, nex)
                wait("vector", sC, nex)
                def fdn(v, j=j, qt=qt, first=(kt == 0)):
                    src = PT[:, j * 512:(j + 1) * 512]
                    d = DN[:, qt * 512:(qt + 1) * 512]
                    if first:
                        return v.tensor_copy(d, src)
                    return v.tensor_tensor(d, d, src, OP.add)
                ndn = run("vector", fdn, sV)
                def mav(t, kt=kt, j=j, pso=pso, first=(kt == 0)):
                    return t.matmul(pso[:, :], Vh[:, kt * 128:(kt + 1) * 128],
                                    PT[:, j * 512:(j + 1) * 512],
                                    start=first, stop=(kt == L // 128 - 1))
                nav = run("tensor", mav, sT)
                wait("scalar", sV, ndn)
                wait("scalar", sT, nav)
        for qt in range(4):
            psd = PSL[7]
            nds = run("tensor", lambda t, qt=qt, psd=psd: t.matmul(
                psd[0:1, :], CONST[:, 2:3], DN[:, qt * 512:(qt + 1) * 512],
                start=True, stop=True), sT)
            wait("vector", sT, nds)
            ndr = run("vector", lambda v, psd=psd: v.tensor_copy(
                DNr[:, :], psd[0:1, :]), sV)
            wait("tensor", sV, ndr)
            psb = PSL[6]
            nbc = run("tensor", lambda t, psb=psb: t.matmul(
                psb[:, :], ONE1[:, :], DNr[:, :], start=True, stop=True), sT)
            wait("vector", sT, nbc)
            pso = PSL[2 + qt]
            def fnorm(v, h=h, qt=qt, pso=pso, psb=psb):
                v.reciprocal(RECB[:, :], psb[:, :])
                return v.tensor_tensor(
                    ATT[:, h * OWN + qt * 512:h * OWN + (qt + 1) * 512],
                    pso[:, :], RECB[:, :], OP.mult)
            nnm = run("vector", fnorm, sV)
            wait("tensor", sV, nnm)
        sT_head_end[h + 1] = sT.n
    emit_meanpass(len(mp_list))
    barrier()

    cnts = [float(D * HID), float(D * HID), float(HID * D)]
    psg = PSL[0]
    ONESC = CONST[:, 2:3]
    for mi in range(3):
        par = mp_par[mi]
        run("tensor", lambda t, mi=mi, par=par: t.matmul(
            psg[0:1, 4 + mi:5 + mi], WACC[:, 4 * par + mi:4 * par + mi + 1],
            ONESC, start=True, stop=True, skip_group_check=True), sT)
    ngm = sT.n
    wait("vector", sT, ngm)
    def fgam(v):
        v.tensor_copy(COL[0:1, 4:7], psg[0:1, 4:7])
        spacer(v)
        for mi in range(3):
            v.tensor_scalar(COL[0:1, 8 + mi:9 + mi], COL[0:1, 4 + mi:5 + mi],
                            1.0 / cnts[mi], None, OP.mult)
        spacer(v)
        for mi in range(3):
            v.tensor_tensor(COL[0:1, mi:mi + 1], COL[0:1, 8 + mi:9 + mi],
                            CONST[0:1, 4:5], OP.max)
        return v.nop()
    nvg = run("vector", fgam, sV)
    wait("tensor", sV, nvg)
    psb = PSL[1]
    nb = run("tensor", lambda t, psb=psb: t.matmul(
        psb[:, 0:3], ONE1[:, :], COL[0:1, 0:3], start=True, stop=True), sT)
    wait("vector", sT, nb)
    def fgam2(v, psb=psb):
        v.tensor_scalar(GAM[:, 0:1], psb[:, 0:1], 1.0 / 127.0, None, OP.mult)
        v.tensor_scalar(GAM[:, 1:2], psb[:, 1:2], 1.0 / 127.0, None, OP.mult)
        v.tensor_scalar(GAM[:, 2:3], psb[:, 2:3], 1.0 / 127.0, None, OP.mult)
        v.tensor_scalar(GAM[:, 3:4], psb[:, 0:1], 0.5, None, OP.mult)
        v.tensor_scalar(GAM[:, 5:6], psb[:, 1:2], 0.5, None, OP.mult)
        v.tensor_scalar(GAM2[:, 0:1], psb[:, 2:3], 0.5, None, OP.mult)
        spacer(v)
        v.tensor_scalar(GAM[:, 4:5], GAM[:, 3:4], -1.0, None, OP.mult)
        v.tensor_scalar(GAM[:, 6:7], GAM[:, 5:6], -1.0, None, OP.mult)
        return v.tensor_scalar(GAM2[:, 1:2], GAM2[:, 0:1], -1.0, None, OP.mult)
    nv2 = run("vector", fgam2, sV)
    wait("tensor", sV, nv2)
    barrier()
    S2a.close()

    S3 = open_stack()
    WOs = sb(S3, "WOs", [128, NH * D], F16)
    XR = sb(S3, "XR", [128, D], F32)
    RT = sb(S3, "RT", [128, D], F32)
    for t in range(NH):
        dma("gpsimd", dG, WOs[:, t * D:(t + 1) * D], wo[t * 128:(t + 1) * 128, :])
    barrier()

    for t in range(NT):
        nx = dma("sync", dL, XR[:, :], xb[t * 128:(t + 1) * 128, :])
        wait("vector", dL, nx)
        for dgrp in range(4):
            pb = PSL[dgrp % 2]
            def mmo(te, t=t, dgrp=dgrp, pb=pb):
                r = None
                for h in range(NH):
                    r = te.matmul(pb[:, :],
                                  ATT[:, h * OWN + t * 128:h * OWN + (t + 1) * 128],
                                  WOs[:, h * D + dgrp * 512:h * D + (dgrp + 1) * 512],
                                  start=(h == 0), stop=(h == NH - 1))
                return r
            nmm = run("tensor", mmo, sT)
            wait("vector", sT, nmm)
            nev = run("vector", lambda v, dgrp=dgrp, pb=pb: v.tensor_tensor(
                RT[:, dgrp * 512:(dgrp + 1) * 512], pb[:, :],
                XR[:, dgrp * 512:(dgrp + 1) * 512], OP.add), sV)
            wait("tensor", sV, nev)
        nrt = sV.n
        wait("scalar", sV, nrt)
        nd = dma("scalar", dS, r_hbm[t * 128:(t + 1) * 128, :], RT[:, :])
        wait("vector", dS, nd)
        wait("sync", dS, nd)
    barrier()
    dbg_snap("r")
    S3.close()
    S2b.close()

    S4b = open_stack()
    XQT = sb(S4b, "XQT", [128, DT * OWN], F16)
    S4a = open_stack()
    LN2 = sb(S4a, "LN2", [128, D], F32)
    V0B = sb(S4a, "V0B", [128, D], F32)
    WKB = sb(S4a, "WKB", [128, D], F32)
    BIT = sb(S4a, "BIT", [128, D], F32)
    XQN = sb(S4a, "XQN", [128, D], F16)
    XR4 = sb(S4a, "XR4", [128, D], F32)

    for t in range(NT):
        nr = dma("sync", dL, XR4[:, :], r_hbm[t * 128:(t + 1) * 128, :])
        wait("vector", dL, nr)
        wait("scalar", dL, nr)
        nln = ln_tile(XR4[:, :], LN2[:, :])
        wait("vector", sC, nln)
        def fgx(v, t=t):
            v.reduce_max(COL[:, 0:1], LN2[:, :], axis=AX.X,
                         apply_absolute_value=True)
            spacer(v)
            v.tensor_tensor(GX[:, t:t + 1], COL[:, 0:1], CONST[:, 4:5],
                            OP.max)
            spacer(v)
            return v.nop()
        run("vector", fgx)
        run("vector", recip_col(COL[:, 1:2], GX[:, t:t + 1], scale=127.0))
        def fq(v):
            spacer(v)
            v.tensor_scalar(V0B[:, :], LN2[:, :], COL[:, 1:2], 128.5,
                            OP.mult, OP.add)
            v.tensor_copy(WKB[:, :], V0B[:, :])
            for k in range(7, -1, -1):
                v.tensor_single_scalar(BIT[:, :], WKB[:, :], float(2 ** k), OP.is_ge)
                v.scalar_tensor_tensor(WKB[:, :], BIT[:, :], float(-(2 ** k)),
                                       WKB[:, :], OP.mult, OP.add)
            v.tensor_tensor(V0B[:, :], V0B[:, :], WKB[:, :], OP.subtract)
            return v.tensor_scalar(XQN[:, :], V0B[:, :], -128.0, None, OP.add)
        nq = run("vector", fq, sV)
        wait("scalar", sV, nq)
        nd = dma("scalar", dS, xq_hbm[t * 128:(t + 1) * 128, :], XQN[:, :])
        wait("vector", dS, nd)
        wait("sync", dS, nd)
    ntr = None
    for c in range(DT):
        ntr = dma("sync", dL, XQT[:, c * OWN:(c + 1) * OWN],
                  xq_hbm[:, c * 128:(c + 1) * 128], transpose=True)
    wait("tensor", dL, ntr)
    wait("vector", dL, ntr)
    if DEBUG:
        dma("sync", dL, dbg_out[:, 112:128], V0B[:, 0:16])
        dma("sync", dL, dbg_out[:, 128:144], WKB[:, 0:16])
        dma("sync", dL, dbg_out[:, 144:160], LN2[:, 0:16])
        dma("sync", dL, dbg_out[:, 160:162], COL[:, 0:2])
    barrier()
    dbg_snap("xq")
    S4a.close()

    S5 = open_stack()
    GWf = sb(S5, "GWf", [128, 2 * 512], F32)
    VWf = sb(S5, "VWf", [128, 2 * 512], F32)
    TGW = sb(S5, "TGW", [128, DT * 512], F16)
    TVW = sb(S5, "TVW", [128, DT * 512], F16)
    GBR = sb(S5, "GBR", [128, HID], F16)
    VBR = sb(S5, "VBR", [128, HID], F16)
    BRS = sb(S5, "BRS", [1, 512], F32)
    GT5 = sb(S5, "GT5", [128, 512], F32)
    VT5 = sb(S5, "VT5", [128, 512], F32)
    GV5 = sb(S5, "GV5", [128, 512], F16)
    GS5 = sb(S5, "GS5", [128, 512], F32)

    for cc2 in range(HID // 512):
        nbr = dma("sync", dL, BRS[:, :], gb[0:1, cc2 * 512:(cc2 + 1) * 512])
        wait("tensor", dL, nbr)
        pb = PSL[cc2 % 2]
        nmm = run("tensor", lambda t, pb=pb: t.matmul(
            pb[:, :], ONE1[:, :], BRS[:, :], start=True, stop=True), sT)
        wait("vector", sT, nmm)
        nev = run("vector", lambda v, cc2=cc2, pb=pb: v.tensor_copy(
            GBR[:, cc2 * 512:(cc2 + 1) * 512], pb[:, :]), sV)
        wait("tensor", sV, nev)
        wait("sync", sV, nev)
    for cc2 in range(HID // 512):
        nbr = dma("sync", dL, BRS[:, :], vb[0:1, cc2 * 512:(cc2 + 1) * 512])
        wait("tensor", dL, nbr)
        pb = PSL[cc2 % 2]
        nmm = run("tensor", lambda t, pb=pb: t.matmul(
            pb[:, :], ONE1[:, :], BRS[:, :], start=True, stop=True), sT)
        wait("vector", sT, nmm)
        nev = run("vector", lambda v, cc2=cc2, pb=pb: v.tensor_copy(
            VBR[:, cc2 * 512:(cc2 + 1) * 512], pb[:, :]), sV)
        wait("tensor", sV, nev)
        wait("sync", sV, nev)
    for cc2 in range(D // 512):
        nbr = dma("sync", dL, BRS[:, :], ob[0:1, cc2 * 512:(cc2 + 1) * 512])
        wait("tensor", dL, nbr)
        pb = PSL[cc2 % 2]
        nmm = run("tensor", lambda t, pb=pb: t.matmul(
            pb[:, :], ONE1[:, :], BRS[:, :], start=True, stop=True), sT)
        wait("vector", sT, nmm)
        nev = run("vector", lambda v, cc2=cc2, pb=pb: v.tensor_copy(
            OUTB[:, cc2 * 512:(cc2 + 1) * 512], pb[:, :]), sV)
        wait("tensor", sV, nev)
        wait("sync", sV, nev)
    def fsg(v):
        for t in range(NT):
            v.tensor_tensor(SGc[:, t:t + 1], GX[:, t:t + 1], GAM[:, 0:1], OP.mult)
            v.tensor_tensor(SVc[:, t:t + 1], GX[:, t:t + 1], GAM[:, 1:2], OP.mult)
        return v.nop()
    run("vector", fsg)

    sT_chunk_end = 0
    for ch in range(HID // 512):
        wait("vector", sT, sT_chunk_end)
        nquant = None
        for c in range(DT):
            p = c % 2
            ng5 = dma("sync", dL, GWf[:, p * 512:(p + 1) * 512],
                      gwT[c * 128:(c + 1) * 128, ch * 512:(ch + 1) * 512])
            nv5 = dma("sync", dL, VWf[:, p * 512:(p + 1) * 512],
                      vwT[c * 128:(c + 1) * 128, ch * 512:(ch + 1) * 512])
            wait("vector", dL, nv5)
            def fquant(v, c=c, p=p):
                gsl = GWf[:, p * 512:(p + 1) * 512]
                v.tensor_scalar(GT5[:, :], gsl, GAM[:, 3:4], None, OP.is_ge)
                v.tensor_scalar(VT5[:, :], gsl, GAM[:, 4:5], None, OP.is_le)
                v.tensor_tensor(TGW[:, c * 512:(c + 1) * 512], GT5[:, :],
                                VT5[:, :], OP.subtract)
                vsl = VWf[:, p * 512:(p + 1) * 512]
                v.tensor_scalar(GT5[:, :], vsl, GAM[:, 5:6], None, OP.is_ge)
                v.tensor_scalar(VT5[:, :], vsl, GAM[:, 6:7], None, OP.is_le)
                return v.tensor_tensor(TVW[:, c * 512:(c + 1) * 512], GT5[:, :],
                                       VT5[:, :], OP.subtract)
            nquant = run("vector", fquant, sV)
            wait("sync", sV, nquant)
        wait("tensor", sV, nquant)
        for t in range(NT):
            pg = PSL[t % 2]
            pv = PSL[2 + t % 2]
            def mmg(te, t=t, pg=pg):
                r = None
                for c in range(DT):
                    r = te.matmul(pg[:, :],
                                  XQT[:, c * OWN + t * 128:c * OWN + (t + 1) * 128],
                                  TGW[:, c * 512:(c + 1) * 512],
                                  start=(c == 0), stop=(c == DT - 1))
                return r
            nmg = run("tensor", mmg, sT)
            def mmv2(te, t=t, pv=pv):
                r = None
                for c in range(DT):
                    r = te.matmul(pv[:, :],
                                  XQT[:, c * OWN + t * 128:c * OWN + (t + 1) * 128],
                                  TVW[:, c * 512:(c + 1) * 512],
                                  start=(c == 0), stop=(c == DT - 1))
                return r
            nmv = run("tensor", mmv2, sT)
            wait("vector", sT, nmg)
            nstg = run("vector", lambda v, t=t, ch=ch, pg=pg: v.scalar_tensor_tensor(
                GT5[:, :], pg[:, :], SGc[:, t:t + 1],
                GBR[:, ch * 512:(ch + 1) * 512], OP.mult, OP.add), sV)
            wait("scalar", sV, nstg)
            nsl = run("scalar", lambda s: s.activation(
                GS5[:, :], GT5[:, :], AF.Silu), sC)
            wait("vector", sT, nmv)
            nstv = run("vector", lambda v, t=t, ch=ch, pv=pv: v.scalar_tensor_tensor(
                VT5[:, :], pv[:, :], SVc[:, t:t + 1],
                VBR[:, ch * 512:(ch + 1) * 512], OP.mult, OP.add), sV)
            wait("tensor", sV, nstv)
            wait("vector", sC, nsl)
            par = ch % 2
            def fgv(v, t=t, par=par):
                v.tensor_tensor(GV5[:, :], GS5[:, :], VT5[:, :], OP.mult)
                v.reduce_max(COL[:, 0:1], GV5[:, :], axis=AX.X,
                             apply_absolute_value=True)
                spacer(v)
                src_m = GMX if par == 0 else GMXb
                dst_m = GMXb if par == 0 else GMX
                return v.tensor_tensor(dst_m[:, t:t + 1], src_m[:, t:t + 1],
                                       COL[:, 0:1], OP.max)
            ngv = run("vector", fgv, sV)
            wait("scalar", sV, ngv)
            nd = dma("scalar", dS,
                     gv_hbm[t * 128:(t + 1) * 128, ch * 512:(ch + 1) * 512],
                     GV5[:, :])
            wait("vector", dS, nd)
        sT_chunk_end = sT.n
    barrier()
    dbg_snap("gv")
    S5.close()
    S4b.close()

    S6 = open_stack()
    TOW = sb(S6, "TOW", [128, NHC * 1024], F16)
    OWf6 = sb(S6, "OWf6", [128, 1024], F32)
    GQT = sb(S6, "GQT", [128, 2 * NHC * 128], F16)
    SRr = sb(S6, "SRr", [1, 512], F32)
    GVT6 = sb(S6, "GVT6", [128, 2 * 512], F16)
    U6 = sb(S6, "U6", [128, 512], F32)
    V06 = sb(S6, "V06", [128, 512], F32)
    W6 = sb(S6, "W6", [128, 512], F32)
    B6 = sb(S6, "B6", [128, 512], F32)
    GQ6 = sb(S6, "GQ6", [128, 512], F16)
    YT = sb(S6, "YT", [128, 1024], F32)
    RT6 = sb(S6, "RT6", [128, 1024], F32)

    def fp6(v):
        for t in range(NT):
            v.tensor_tensor(GMX2[:, t:t + 1], GMX[:, t:t + 1], CONST[:, 4:5],
                            OP.max)
        spacer(v)
        for rep in range(4):
            v.tensor_copy(TRASH[:, rep * 16:(rep + 1) * 16], GMX2[:, :])
        spacer(v)
        v.reciprocal(TRASH[:, 64:128], TRASH[:, 0:64])
        spacer(v)
        v.tensor_scalar(SQc[:, :], TRASH[:, 64:80], 127.0, None, OP.mult)
        for t in range(NT):
            v.tensor_tensor(SOUTc[:, t:t + 1], GMX2[:, t:t + 1], GAM[:, 2:3],
                            OP.mult)
        return v.nop()
    np6 = run("vector", fp6, sV)
    wait("tensor", sV, np6)
    wait("scalar", sV, np6)

    for i in range(NHC):
        for colh in range(2):
            no = dma("sync", dL, OWf6[:, :],
                     owT[i * 128:(i + 1) * 128, colh * 1024:(colh + 1) * 1024])
            wait("vector", dL, no)
            def fqo(v, i=i, colh=colh):
                v.tensor_scalar(GT5[:, :], OWf6[:, 0:512], GAM2[:, 0:1], None,
                                OP.is_ge)
                v.tensor_scalar(VT5[:, :], OWf6[:, 0:512], GAM2[:, 1:2], None,
                                OP.is_le)
                dst1 = TOW[:, i * 1024:i * 1024 + 512] if colh == 0 else GV5[:, :]
                v.tensor_tensor(dst1, GT5[:, :], VT5[:, :], OP.subtract)
                v.tensor_scalar(GT5[:, :], OWf6[:, 512:1024], GAM2[:, 0:1], None,
                                OP.is_ge)
                v.tensor_scalar(VT5[:, :], OWf6[:, 512:1024], GAM2[:, 1:2], None,
                                OP.is_le)
                dst2 = (TOW[:, i * 1024 + 512:(i + 1) * 1024] if colh == 0
                        else GQ6[:, :])
                return v.tensor_tensor(dst2, GT5[:, :], VT5[:, :], OP.subtract)
            nq = run("vector", fqo, sV)
            wait("sync", sV, nq)
            if colh == 1:
                wait("scalar", sV, nq)
                nd = dma("scalar", dS, tow1_hbm[:, i * 1024:i * 1024 + 512],
                         GV5[:, :])
                nd = dma("scalar", dS,
                         tow1_hbm[:, i * 1024 + 512:(i + 1) * 1024], GQ6[:, :])
                wait("vector", dS, nd)

    nfu_end = [0, 0]
    for gq in range(4):
        psr = PSL[7]
        for s4 in range(4):
            t = gq * 4 + s4
            run("tensor", lambda te, t=t, s4=s4, psr=psr: te.matmul(
                psr[0:1, s4 * 128:(s4 + 1) * 128], SQc[:, t:t + 1], IDEN[:, :],
                start=(s4 == 0), stop=(s4 == 3)), sT)
        nsrow = sT.n
        wait("vector", sT, nsrow)
        nsr = run("vector", lambda v, psr=psr: v.tensor_copy(
            SRr[:, :], psr[0:1, :]), sV)
        wait("tensor", sV, nsr)
        psb6 = PSL[6]
        nbc = run("tensor", lambda t, psb6=psb6: t.matmul(
            psb6[:, :], ONE1[:, :], SRr[:, :], start=True, stop=True), sT)
        wait("vector", sT, nbc)
        for i in range(NHC):
            p = i % 2
            wait("sync", sV, nfu_end[p])
            ngt = dma("sync", dL, GVT6[:, p * 512:(p + 1) * 512],
                      gv_hbm[gq * 512:(gq + 1) * 512, i * 128:(i + 1) * 128],
                      transpose=True)
            wait("vector", dL, ngt)
            def fu(v, p=p, psb6=psb6):
                v.tensor_tensor(U6[:, :], GVT6[:, p * 512:(p + 1) * 512],
                                psb6[:, :], OP.mult)
                v.tensor_scalar(V06[:, :], U6[:, :], 128.5, None, OP.add)
                return v.tensor_copy(W6[:, :], V06[:, :])
            nfu = run("vector", fu, sV)
            nfu_end[p] = nfu
            wait("scalar", sV, nfu)
            for k in range(7, -1, -1):
                nsg2 = run("scalar", lambda s: s.activation(
                    B6[:, :], W6[:, :], AF.Sign, bias=CONST[:, 3:4],
                    scale=CONST[:, 2:3]), sC)
                wait("vector", sC, nsg2)
                coeff = -(2.0 ** (k - 1))
                npl = run("vector", lambda v, coeff=coeff: v.scalar_tensor_tensor(
                    W6[:, :], B6[:, :], coeff, W6[:, :], OP.mult, OP.add), sV)
                if k > 0:
                    wait("scalar", sV, npl)
            def ffin(v):
                v.tensor_tensor(V06[:, :], V06[:, :], W6[:, :], OP.subtract)
                return v.tensor_scalar(GQ6[:, :], V06[:, :], -0.5, None, OP.add)
            nfi = run("vector", ffin, sV)
            wait("scalar", sV, nfi)
            nd = dma("scalar", dS,
                     gvq_hbm[:, i * OWN + gq * 512:i * OWN + (gq + 1) * 512],
                     GQ6[:, :])
            wait("vector", dS, nd)
        wait("tensor", dS, dS.n)
    barrier()
    dbg_snap("gvq")

    for half in range(2):
        if half == 1:
            wait("sync", sT, sT.n)
            ntw = dma("sync", dL, TOW[:, :], tow1_hbm[:, :])
            wait("tensor", dL, ntw)
        ngq = [0, 0]
        sT_tile_end = [0] * (NT + 2)

        def load_gq(t):
            p = t % 2
            src = gvq_hbm[:, :].rearrange("p (i w) -> p i w", i=NHC)[
                :, :, t * 128:(t + 1) * 128]
            dstv = GQT[:, p * NHC * 128:(p + 1) * NHC * 128].rearrange(
                "p (i w) -> p i w", i=NHC)
            return dma("sync", dL, dstv, src)

        for t in range(NT):
            p = t % 2
            wait("sync", sT, sT_tile_end[t])
            ngq[p] = load_gq(t)
            nr = dma("sync", dL, RT6[:, :],
                     r_hbm[t * 128:(t + 1) * 128, half * 1024:(half + 1) * 1024])
            wait("tensor", dL, ngq[p])
            for dgrp in range(2):
                pb = PSL[dgrp]
                def mmo6(te, p=p, dgrp=dgrp, pb=pb):
                    r = None
                    for i in range(NHC):
                        r = te.matmul(
                            pb[:, :],
                            GQT[:, p * NHC * 128 + i * 128:p * NHC * 128 + (i + 1) * 128],
                            TOW[:, i * 1024 + dgrp * 512:i * 1024 + (dgrp + 1) * 512],
                            start=(i == 0), stop=(i == NHC - 1))
                    return r
                nmm = run("tensor", mmo6, sT)
                wait("vector", sT, nmm)
                wait("vector", dL, nr)
                def fep(v, t=t, dgrp=dgrp, half=half, pb=pb):
                    c6 = dgrp * 512
                    v.scalar_tensor_tensor(YT[:, c6:c6 + 512], pb[:, :],
                                           SOUTc[:, t:t + 1],
                                           OUTB[:, half * 1024 + c6:
                                                half * 1024 + c6 + 512],
                                           OP.mult, OP.add)
                    return v.tensor_tensor(YT[:, c6:c6 + 512], YT[:, c6:c6 + 512],
                                           RT6[:, c6:c6 + 512], OP.add)
                nep = run("vector", fep, sV)
                wait("tensor", sV, nep)
            sT_tile_end[t + 2] = sT.n
            nyv = sV.n
            wait("scalar", sV, nyv)
            nd = dma("scalar", dS,
                     y_out[t * 128:(t + 1) * 128, half * 1024:(half + 1) * 1024],
                     YT[:, :])
            wait("vector", dS, nd)
            wait("sync", dS, nd)
    barrier()
    if DEBUG:
        dma("sync", dL, dbg_out[:, 0:8], GAM[:, :])
        dma("sync", dL, dbg_out[:, 8:10], GAM2[:, :])
        dma("sync", dL, dbg_out[:, 200:208], WACC[:, :])
        dma("sync", dL, dbg_out[:, 16:32], GX[:, :])
        dma("sync", dL, dbg_out[:, 32:48], SGc[:, :])
        dma("sync", dL, dbg_out[:, 48:64], SVc[:, :])
        dma("sync", dL, dbg_out[:, 64:80], GMX[:, :])
        dma("sync", dL, dbg_out[:, 80:96], SQc[:, :])
        dma("sync", dL, dbg_out[:, 96:112], SOUTc[:, :])
        def fdone(eng):
            eng.wait_ge(dL.h, dL.n)
        emit("sync", fdone)
    barrier()

    with root:
        blk = root.enter_context(nc.Block())
        @blk.sync
        def _(eng):
            for f in sched["sync"]:
                f(eng)
        @blk.tensor
        def _(eng):
            for f in sched["tensor"]:
                f(eng)
        @blk.vector
        def _(eng):
            for f in sched["vector"]:
                f(eng)
        @blk.scalar
        def _(eng):
            for f in sched["scalar"]:
                f(eng)
        @blk.gpsimd
        def _(eng):
            for f in sched["gpsimd"]:
                f(eng)
        for s in reversed(phase_stacks):
            s.close()
    return nc


def _host_prep(inputs):
    x = np.ascontiguousarray(inputs["L0_attn_x"])
    f32 = np.float32
    shared = {
        "wdq": np.ascontiguousarray(inputs["L0_attn_W_D_Q"], dtype=f32),
        "wuq": np.ascontiguousarray(inputs["L0_attn_W_U_Q"], dtype=f32),
        "wdkv": np.ascontiguousarray(inputs["L0_attn_W_D_KV"], dtype=f32),
        "wuk": np.ascontiguousarray(inputs["L0_attn_W_U_K"], dtype=f32),
        "wuv": np.ascontiguousarray(inputs["L0_attn_W_U_V"], dtype=f32),
        "wo": np.ascontiguousarray(inputs["L0_attn_W_O"], dtype=f32),
        "gwT": np.ascontiguousarray(np.asarray(inputs["L0_ffn_gate_w"]).T),
        "vwT": np.ascontiguousarray(np.asarray(inputs["L0_ffn_val_w"]).T),
        "owT": np.ascontiguousarray(np.asarray(inputs["L0_ffn_out_w"]).T),
        "gb": np.ascontiguousarray(np.asarray(inputs["L0_ffn_gate_b"]).reshape(1, HID), dtype=f32),
        "vb": np.ascontiguousarray(np.asarray(inputs["L0_ffn_val_b"]).reshape(1, HID), dtype=f32),
        "ob": np.ascontiguousarray(np.asarray(inputs["L0_ffn_out_b"]).reshape(1, D), dtype=f32),
    }
    permA = (np.arange(NH)[:, None] * DHR + np.arange(32)[None, :]).reshape(-1)
    permB = (np.arange(NH)[:, None] * DHR + 32 + np.arange(32)[None, :]).reshape(-1)
    perm = np.concatenate([permA, permB])
    shared["wqr"] = np.ascontiguousarray(np.asarray(inputs["L0_attn_W_Q_R"])[:, perm], dtype=f32)
    shared["wkr"] = np.ascontiguousarray(np.asarray(inputs["L0_attn_W_K_R"])[:, perm], dtype=f32)

    inv_freq = 10000.0 ** (-(np.arange(0, DHR, 2, dtype=np.float64)) / DHR)
    pos = np.arange(L, dtype=np.float64)
    ang = pos[None, :] * inv_freq[:, None]
    cosk = np.tile(np.cos(ang), (4, 1)).astype(f32)
    sink = np.tile(np.sin(ang), (4, 1)).astype(f32)

    in_maps = []
    for core in range(8):
        b, qh = core // 2, core % 2
        m = dict(shared)
        xbv = np.asarray(x[b], dtype=f32)
        if qh == 1:
            xbv = np.concatenate([xbv[OWN:], xbv[:OWN]], axis=0)
            ck = np.concatenate([cosk[:, OWN:], cosk[:, :OWN]], axis=1)
            sk = np.concatenate([sink[:, OWN:], sink[:, :OWN]], axis=1)
        else:
            ck, sk = cosk, sink
        m["xb"] = np.ascontiguousarray(xbv)
        m["cosk"] = np.ascontiguousarray(ck)
        m["sink"] = np.ascontiguousarray(sk)
        m["cosq"] = np.ascontiguousarray(ck[:, 0:OWN])
        m["sinq"] = np.ascontiguousarray(sk[:, 0:OWN])
        in_maps.append(m)
    return in_maps


_NC_CACHE = {}


def kernel(**inputs):
    in_maps = _host_prep(inputs)
    if "nc" not in _NC_CACHE:
        _NC_CACHE["nc"] = build_program()
    nc = _NC_CACHE["nc"]
    res = run_bass_kernel_spmd(nc, in_maps, core_ids=list(range(8)))
    y = np.empty((B, L, D), dtype=np.float32)
    for core in range(8):
        b, qh = core // 2, core % 2
        y[b, qh * OWN:(qh + 1) * OWN, :] = np.asarray(res.results[core]["y"])
    return y
